# version 03
# baseline (speedup 1.0000x reference)
import numpy as np
from contextlib import ExitStack

import concourse.bass as bass
import concourse.mybir as mybir
import concourse.bass_utils as _bu
from concourse.bass_utils import run_bass_kernel_spmd

F32 = mybir.dt.float32
U8 = mybir.dt.uint8
BF16 = mybir.dt.bfloat16
AF = mybir.ActivationFunctionType
ALU = mybir.AluOpType
AX = mybir.AxisListType

D = 2048
T = 2048
S = 4096
NCORES = 8
HA, DHA = 32, 64
NHB, DHB, RK = 16, 128, 32
HF = 2730
NE = 8
EPS = 1e-5
HCH = [(128 * i, 128) for i in range(21)] + [(2688, 42)]
NIC = D // 128
NTC = T // 512
NTT = T // 128

PHASES = ["setup", "ln1", "aproj", "acore", "aout", "ln2", "ffn1",
          "ln3", "bprep", "bcore", "bout", "ln4", "ffn2"]
TRUNCATE = None

AW = 6144
MW = 6016
MOFF = 3968


def _fold_cum(lb):
    c = 0
    for j in range(lb):
        c += 4 if 32 <= j < 48 else 16
    return c


class Sem:
    def __init__(self, nc, name):
        self.h = nc.alloc_semaphore(name)
        self.v = 0

    def inc(self, inst, n=1):
        inst.then_inc(self.h, n)
        self.v += n
        return inst

    def wait(self, eng, v=None):
        eng.wait_ge(self.h, self.v if v is None else v)


class MK:
    def __init__(self, nc):
        self.nc = nc
        self.ctx = ExitStack()
        self.sems = {}

    def sem(self, name):
        if name not in self.sems:
            self.sems[name] = Sem(self.nc, name)
        return self.sems[name]


def build_nc(trunc=None):
    nc = bass.Bass("TRN2", target_bir_lowering=False)
    mk = MK(nc)
    upto = PHASES.index(trunc) if trunc else len(PHASES) - 1

    ei = lambda n, s: nc.dram_tensor(n, s, F32, kind="ExternalInput")
    xin = ei("xin", [D, T])
    wqT = ei("wqT", [D, D]); wkT = ei("wkT", [D, D])
    wvT = ei("wvT", [D, D]); woT = ei("woT", [D, D])
    relT = ei("relT", [64, 1023])
    halfin = ei("halfin", [1, 1])
    iden = ei("iden", [128, 128])

    need = lambda name: PHASES.index(name) <= upto
    if need("bprep"):
        bWq = ei("bWq", [D, D]); bWk = ei("bWk", [D, D]); bWv = ei("bWv", [D, D])
        bWoT = ei("bWoT", [D, D])
        wdn = ei("wdn", [DHB, RK])
        wupT = ei("wupT", [DHB, RK])
    if need("ffn1"):
        NH_ = len(HCH)
        sw1R = ei("sw1R", [NH_, 128, 2048]); sw3R = ei("sw3R", [NH_, 128, 2048])
        sw2R = ei("sw2R", [16, 128, NH_ * 128])
        ew1R = ei("ew1R", [NE, NH_, 128, 2048]); ew3R = ei("ew3R", [NE, NH_, 128, 2048])
        ew2R = ei("ew2R", [NE, 16, 128, NH_ * 128])
        wrTp = ei("wrTp", [D, 64])

    tout = nc.dram_tensor("tout", [D, T], F32, kind="ExternalOutput")

    di = lambda n, s, dt=BF16: nc.dram_tensor(n, s, dt, kind="Internal")
    AD = di("AD", [1, AW])
    QD = di("QD", [D, T])
    KVIN = di("KVIN", [2 * D, T]); KVOUT = di("KVOUT", [4 * D, T])
    CTXD = di("CTXD", [D, T])
    if need("bprep"):
        QRD = di("QRD", [1024, T])
        KRIN = di("KRIN", [1024, T]); KROUT = di("KROUT", [2048, T])
        VRIN = di("VRIN", [NHB * T, 33]); VROUT = di("VROUT", [2 * NHB * T, 33])
        WQRD = di("WQRD", [D, 1024]); WKRD = di("WKRD", [D, 1024])
        WVRD = di("WVRD", [D, 512])
        WOFD = di("WOFD", [1024, D])

    sbp = lambda n, s, d: mk.ctx.enter_context(nc.sbuf_tensor(n, s, d))
    x = [sbp(f"x{i}", [128, T], BF16) for i in range(NIC)]
    xln = [sbp(f"xln{i}", [128, T], BF16) for i in range(NIC)]
    ones128 = sbp("ones128", [128, 1], BF16)
    oheads = sbp("oheads", [128, 2], BF16)
    onesPf = sbp("onesPf", [1, 128], F32)
    ones64f = sbp("ones64f", [64, 1], F32)
    i128 = sbp("i128", [128, 128], BF16)
    o128 = sbp("o128", [128, 128], BF16)

    sx = mk.sem("sx")
    sc0 = mk.sem("sc0")
    ev0 = mk.sem("ev0")
    mmv = mk.sem("mmv")
    ac0 = mk.sem("ac0")
    st0 = mk.sem("st0")

    with ExitStack() as pctx:
        sb = lambda n, s, d: pctx.enter_context(nc.sbuf_tensor(n, s, d))
        texp = sb("texp", [1, 1024], F32)
        texc = sb("texc", [1, 1023], BF16)
        fill_hi = sb("fill_hi", [1, 3584], BF16)
        fill_lo = sb("fill_lo", [1, 2048], BF16)
        onesrow = sb("onesrow", [1, 3584], BF16)
        maskr = sb("maskr", [1, 1536], U8)
        arow = sb("arow", [1, AW], BF16)
        relsb = sb("relsb", [64, 1023], F32)
        halfsb = sb("halfsb", [1, 1], F32)
        ps_t8 = pctx.enter_context(nc.psum_tensor("ps_t8", [1, 512], F32))
        ps_t8b = pctx.enter_context(nc.psum_tensor("ps_t8b", [1, 512], F32))

        with nc.Block("setup") as blk:
            @blk.gpsimd
            def _(g):
                for i in range(NIC):
                    sx.inc(g.dma_start(
                        x[i][:], xin.rearrange("(c p) t -> c p t", p=128)[i]), 16)
                sx.inc(g.dma_start(i128[:], iden[:]), 16)
                sx.wait(g)

            @blk.sync
            def _(sy):
                sc0.inc(sy.dma_start(relsb[:], relT[:]), 16)
                sc0.inc(sy.dma_start(halfsb[:], halfin[:]), 16)

            @blk.vector
            def _(v):
                v.memset(ones128[:], 1.0)
                v.memset(oheads[0:64, 0:1], 1.0)
                v.memset(oheads[64:128, 0:1], 0.0)
                v.memset(oheads[0:64, 1:2], 0.0)
                v.memset(oheads[64:128, 1:2], 1.0)
                v.memset(onesPf[:], 1.0)
                v.memset(o128[:], 1.0)
                v.memset(ones64f[:], 1.0 / 64.0)
                ev0.inc(v.memset(onesrow[:], 1.0))

            @blk.tensor
            def _(t):
                ev0.wait(t)
                sc0.wait(t)
                t.matmul(ps_t8[:], ones64f[:], relsb[:, 0:512], start=True, stop=True)
                mmv.inc(t.matmul(ps_t8b[0:1, 0:511], ones64f[:], relsb[:, 512:1023],
                                 start=True, stop=True))

            @blk.scalar
            def _(s):
                mmv.wait(s)
                s.activation(texp[0:1, 0:512], ps_t8[:], AF.Exp)
                ac0.inc(s.activation(texp[0:1, 512:1023], ps_t8b[0:1, 0:511], AF.Exp))

            @blk.vector
            def _(v):
                ac0.wait(v)
                v.tensor_copy(texc[:], texp[0:1, 0:1023])
                hi_ap = bass.AP(texp, 1022, [[1024, 1], [1, 1]])
                lo_ap = bass.AP(texp, 0, [[1024, 1], [1, 1]])
                v.tensor_scalar(fill_hi[:], onesrow[:], hi_ap, None, ALU.mult)
                v.tensor_scalar(fill_lo[:], onesrow[0:1, 0:2048], lo_ap, None, ALU.mult)
                sc0.wait(v)
                ev0.inc(v.tensor_scalar(maskr[:], onesrow[0:1, 0:1536], halfsb[:], None, ALU.mult))
                ev0.wait(v)
                v.tensor_copy(arow[0:1, 0:1537], fill_lo[0:1, 0:1537])
                v.tensor_copy(arow[0:1, 1537:2560], texc[:])
                v.tensor_copy(arow[0:1, 2560:4608], fill_hi[0:1, 0:2048])
                ev0.inc(v.tensor_copy(arow[0:1, 4608:AW], fill_hi[0:1, 0:AW - 4608]))
                ev0.wait(v)
                v.copy_predicated(arow[0:1, 1537:2560], maskr[0:1, 0:1023], fill_lo[0:1, 0:1023])
                v.copy_predicated(arow[0:1, 2560:3585], maskr[0:1, 0:1025], fill_lo[0:1, 0:1025])
                ev0.inc(v.copy_predicated(arow[0:1, 3585:4608], maskr[0:1, 0:1023], texc[:]))

            @blk.sync
            def _(sy):
                ev0.wait(sy)
                st0.inc(sy.dma_start(AD[:], arow[:]), 16)
                st0.wait(sy)

    if not PHASES.index("ln1") <= upto:
        _finish(nc, mk, x, tout)
        mk.ctx.close()
        return nc

    def emit_ln(tag, src, dst):
        sql = mk.sem("sql")
        mml = mk.sem("mml")
        evl = mk.sem("evl")
        acl = mk.sem("acl")
        b_sql, b_mml = sql.v, mml.v
        with ExitStack() as pctx:
            sb = lambda n, s, d: pctx.enter_context(nc.sbuf_tensor(n, s, d))
            xsq = [sb(f"xsq{tag}{i}", [128, T], BF16) for i in range(2)]
            mrow = sb(f"mrow{tag}", [1, T], F32)
            m2row = sb(f"m2row{tag}", [1, T], F32)
            vrow = sb(f"vrow{tag}", [1, T], F32)
            lnr = sb(f"lnr{tag}", [1, T], F32)
            arw = sb(f"arw{tag}", [1, T], F32)
            brw = sb(f"brw{tag}", [1, T], F32)
            ps = [pctx.enter_context(nc.psum_tensor(f"ps{tag}{i}", [128, 512], F32))
                  for i in range(8)]

            with nc.Block(f"ln{tag}") as blk:
                @blk.scalar
                def _(s):
                    for i in range(NIC):
                        if i >= 2:
                            s.wait_ge(mml.h, b_mml + i - 1)
                        sql.inc(s.activation(xsq[i % 2][:], src[i][:], AF.Square))

                @blk.tensor
                def _(t):
                    for i in range(NIC):
                        t.wait_ge(sql.h, b_sql + i + 1)
                        for tc in range(NTC):
                            t.matmul(ps[tc][0:1, :], ones128[:], src[i][:, tc * 512:(tc + 1) * 512],
                                     start=(i == 0), stop=(i == NIC - 1))
                        inst = None
                        for tc in range(NTC):
                            inst = t.matmul(ps[4 + tc][0:1, :], ones128[:],
                                            xsq[i % 2][:, tc * 512:(tc + 1) * 512],
                                            start=(i == 0), stop=(i == NIC - 1))
                        mml.inc(inst)

                @blk.vector
                def _(v):
                    mml.wait(v)
                    inst = None
                    for tc in range(NTC):
                        sl = slice(tc * 512, (tc + 1) * 512)
                        v.tensor_scalar(mrow[0:1, sl], ps[tc][0:1, :], 1.0 / D, None, ALU.mult)
                        inst = v.tensor_scalar(vrow[0:1, sl], ps[4 + tc][0:1, :], 1.0 / D, EPS,
                                               ALU.mult, ALU.add)
                    evl.inc(inst)

                @blk.scalar
                def _(s):
                    evl.wait(s)
                    acl.inc(s.activation(m2row[:], mrow[:], AF.Square))

                @blk.vector
                def _(v):
                    acl.wait(v)
                    evl.inc(v.tensor_sub(vrow[:], vrow[:], m2row[:]))

                @blk.scalar
                def _(s):
                    evl.wait(s)
                    acl.inc(s.activation(lnr[:], vrow[:], AF.Ln))
                    acl.wait(s)
                    acl.inc(s.activation(arw[:], lnr[:], AF.Exp, scale=-0.5))

                @blk.vector
                def _(v):
                    acl.wait(v)
                    evl.inc(v.scalar_tensor_tensor(brw[:], mrow[:], -1.0, arw[:],
                                                   ALU.mult, ALU.mult))

                @blk.tensor
                def _(t):
                    evl.wait(t)
                    inst = None
                    for tc in range(NTC):
                        sl = slice(tc * 512, (tc + 1) * 512)
                        t.matmul(ps[tc][:], onesPf[:], arw[0:1, sl], start=True, stop=True)
                        inst = t.matmul(ps[4 + tc][:], onesPf[:], brw[0:1, sl], start=True, stop=True)
                    mml.inc(inst)

                @blk.vector
                def _(v):
                    mml.wait(v)
                    inst = None
                    for i in range(NIC):
                        for tc in range(NTC):
                            sl = slice(tc * 512, (tc + 1) * 512)
                            inst = v.tensor_tensor(dst[i][:, sl], src[i][:, sl], ps[tc][:], ALU.mult)
                    evl.inc(inst)
                    evl.wait(v)
                    for i in range(NIC):
                        for tc in range(NTC):
                            sl = slice(tc * 512, (tc + 1) * 512)
                            v.tensor_tensor(dst[i][:, sl], dst[i][:, sl], ps[4 + tc][:], ALU.add)

    emit_ln("1", x, xln)
    if upto == PHASES.index("ln1"):
        _finish(nc, mk, xln, tout)
        mk.ctx.close()
        return nc

    def emit_aproj():
        swq = [mk.sem("swq0"), mk.sem("swq1")]
        mma = mk.sem("mma")
        eva = mk.sem("eva")
        stq = [mk.sem("stq0"), mk.sem("stq1")]
        stk = [mk.sem("stk0"), mk.sem("stk1")]
        stv = [mk.sem("stv0"), mk.sem("stv1")]
        ccv = mk.sem("ccv")
        b_mma, b_eva = mma.v, eva.v
        b_sw = [swq[0].v, swq[1].v]
        b_stq = [stq[0].v, stq[1].v]
        b_stk = [stk[0].v, stk[1].v]
        b_stv = [stv[0].v, stv[1].v]
        srcs = [wqT, wkT, wvT]
        NQ = 12

        def gid(qidx, j):
            return qidx * 16 + j

        with ExitStack() as pctx:
            sb = lambda n, s, d: pctx.enter_context(nc.sbuf_tensor(n, s, d))
            wbuf = [sb(f"wpa{i}", [128, 16 * 512], BF16) for i in range(2)]
            qst = [sb(f"qsta{i}", [128, T], BF16) for i in range(2)]
            vst = [sb(f"vsta{i}", [128, 512], BF16) for i in range(2)]
            pp = [pctx.enter_context(nc.psum_tensor(f"ppa{i}", [128, 512], F32))
                  for i in range(4)]

            with nc.Block("aproj") as blk:
                @blk.gpsimd
                def _(g):
                    for qidx in range(NQ):
                        w, oq = qidx // 4, qidx % 4
                        if qidx >= 2:
                            g.wait_ge(mma.h, b_mma + gid(qidx - 2, 16))
                        swq[qidx % 2].inc(g.dma_start(
                            wbuf[qidx % 2][:],
                            srcs[w][:, oq * 512:(oq + 1) * 512]
                            .rearrange("(ic p) o -> p ic o", p=128)), 16)

                @blk.tensor
                def _(t):
                    for qidx in range(NQ):
                        w, oq = qidx // 4, qidx % 4
                        t.wait_ge(swq[qidx % 2].h, b_sw[qidx % 2] + 16 * (qidx // 2 + 1))
                        for j in range(16):
                            g_ = gid(qidx, j)
                            bank = g_ % 4
                            if g_ >= 4:
                                t.wait_ge(eva.h, b_eva + g_ - 3)
                            inst = None
                            if w < 2:
                                oloc, tc = j // 4, j % 4
                                for ic in range(NIC):
                                    co = ic * 512 + oloc * 128
                                    inst = t.matmul(
                                        pp[bank][:], wbuf[qidx % 2][:, co:co + 128],
                                        xln[ic][:, tc * 512:(tc + 1) * 512],
                                        start=(ic == 0), stop=(ic == NIC - 1))
                            else:
                                tt = j
                                for ic in range(NIC):
                                    inst = t.matmul(
                                        pp[bank][:], xln[ic][:, tt * 128:(tt + 1) * 128],
                                        wbuf[qidx % 2][:, ic * 512:(ic + 1) * 512],
                                        start=(ic == 0), stop=(ic == NIC - 1))
                            mma.inc(inst)

                @blk.scalar
                def _(s):
                    for qidx in range(NQ):
                        w, oq = qidx // 4, qidx % 4
                        for j in range(16):
                            g_ = gid(qidx, j)
                            s.wait_ge(mma.h, b_mma + g_ + 1)
                            if w < 2:
                                oloc, tc = j // 4, j % 4
                                oc = oq * 4 + oloc
                                if tc == 0:
                                    if w == 0 and oc >= 2:
                                        s.wait_ge(stq[oc % 2].h, b_stq[oc % 2] + 16 * (oc // 2))
                                    elif w == 1:
                                        if oc < 2:
                                            s.wait_ge(stq[oc % 2].h, b_stq[oc % 2] + 16 * 8)
                                        else:
                                            s.wait_ge(stk[oc % 2].h, b_stk[oc % 2] + 16 * (oc // 2))
                                eva.inc(s.copy(qst[oc % 2][:, tc * 512:(tc + 1) * 512],
                                               pp[g_ % 4][:]))
                            else:
                                sv = (qidx - 8) * 16 + j
                                if sv >= 2:
                                    s.wait_ge(stv[sv % 2].h, b_stv[sv % 2] + 16 * (sv // 2))
                                eva.inc(s.copy(vst[sv % 2][:], pp[g_ % 4][:]))

                @blk.sync
                def _(sy):
                    for qidx in range(NQ):
                        w, oq = qidx // 4, qidx % 4
                        for j in range(16):
                            g_ = gid(qidx, j)
                            if w < 2:
                                oloc, tc = j // 4, j % 4
                                if tc != 3:
                                    continue
                                oc = oq * 4 + oloc
                                sy.wait_ge(eva.h, b_eva + g_ + 1)
                                if w == 0:
                                    stq[oc % 2].inc(sy.dma_start(
                                        QD.rearrange("(c p) t -> c p t", p=128)[oc],
                                        qst[oc % 2][:]), 16)
                                else:
                                    stk[oc % 2].inc(sy.dma_start(
                                        KVIN.rearrange("(c p) t -> c p t", p=128)[oc],
                                        qst[oc % 2][:]), 16)
                            else:
                                tt = j
                                sv = (qidx - 8) * 16 + j
                                sy.wait_ge(eva.h, b_eva + g_ + 1)
                                stv[sv % 2].inc(sy.dma_start(
                                    KVIN[D + tt * 128:D + (tt + 1) * 128,
                                         oq * 512:(oq + 1) * 512],
                                    vst[sv % 2][:]), 16)

                @blk.gpsimd
                def _(g):
                    stk[0].wait(g); stk[1].wait(g)
                    stv[0].wait(g); stv[1].wait(g)
                    ccv.inc(g.collective_compute(
                        "AllGather", ALU.bypass,
                        replica_groups=[[0, 1], [2, 3], [4, 5], [6, 7]],
                        ins=[KVIN.ap().opt()], outs=[KVOUT.ap().opt()]))

    emit_aproj()
    if upto == PHASES.index("aproj"):
        _finish(nc, mk, x, tout)
        mk.ctx.close()
        return nc

    def emit_acore():
        ldh = [mk.sem("ldh0"), mk.sem("ldh1")]
        stm = mk.sem("stm")
        sqs = mk.sem("sqs")
        bse = mk.sem("bse")
        cte = mk.sem("cte")
        ms3 = mk.sem("ms3")
        dvs = mk.sem("dvs")
        mmq = mk.sem("mmq")
        mmav = mk.sem("mmav")
        ac3 = mk.sem("ac3")
        dv3 = mk.sem("dv3")
        rep = mk.sem("rep")
        bcm = mk.sem("bcm")
        epc = mk.sem("epc")
        ctm = mk.sem("ctm")
        von = mk.sem("von")
        ccv = mk.sems["ccv"]
        RING = 6
        b3 = {n: mk.sems[n].v for n in
              ("sqs", "bse", "cte", "ms3", "dvs", "mmq", "mmav", "ac3", "dv3",
               "rep", "bcm", "epc", "ctm")}
        b_ldh = [ldh[0].v, ldh[1].v]

        with ExitStack() as pctx:
            sb = lambda n, s, d: pctx.enter_context(nc.sbuf_tensor(n, s, d))
            M = sb("Mst", [128, MW + 1], BF16)
            ktile = [sb(f"kt3{i}", [128, S], BF16) for i in range(2)]
            qtile = [sb(f"qt3{i}", [128, T], BF16) for i in range(2)]
            vbuf = [[sb(f"vb3{i}{h}", [128, 32 * 65], BF16) for h in range(2)]
                    for i in range(2)]
            sqt = sb("sqt3", [128, T], BF16)
            probs = sb("probs3", [128, 3 * 1024], BF16)
            ctxr = sb("ctxr3", [64, 512], BF16)
            ctx_st = [sb(f"cst3{i}", [128, 512], BF16) for i in range(4)]
            mstat = sb("mstat3", [1, 40], F32)
            rrow = sb("rrow3", [1, 512], F32)
            bias_h = [sb(f"bh3{i}", [128, 1], F32) for i in range(2)]
            qk_ps = [pctx.enter_context(nc.psum_tensor(f"qk3{i}", [128, 1024], F32))
                     for i in range(2)]
            av_ps = pctx.enter_context(nc.psum_tensor("av3", [65, 512], F32))
            bc_ps = pctx.enter_context(nc.psum_tensor("bc3", [64, 512], F32))
            st_ps = [qk_ps[0][0:1, 0:512], qk_ps[0][0:1, 512:1024],
                     qk_ps[1][0:1, 0:512], qk_ps[1][0:1, 512:1024]]
            stb_ps = [qk_ps[0][0:128, 0:1], qk_ps[1][0:128, 0:1]]

            ROUNDS = [(0, 0, 0), (1, 0, 12), (0, 1, 4), (1, 1, 16), (0, 2, 8), (1, 2, 20)]

            def widx(hp, hloc, qc, kt=0):
                return hp * 256 + (hloc * 4 + qc) * 32 + kt

            def grp(hp, hloc, qc):
                return hp * 8 + hloc * 4 + qc

            with nc.Block("acore") as blk:
                @blk.sync
                def _(sy):
                    stm.inc(sy.dma_start(M[:], bass.AP(AD, 0, [[1, 128], [1, MW + 1]])), 16)

                    def stores(hp):
                        sy.wait_ge(epc.h, b3["epc"] + hp * 8 + 8)
                        for qc in range(4):
                            ctm.inc(sy.dma_start(
                                CTXD[hp * 128:(hp + 1) * 128, qc * 512:(qc + 1) * 512],
                                ctx_st[qc][:]), 16)

                    for hp in range(NTT):
                        par = hp % 2
                        if hp >= 2:
                            sy.wait_ge(mmav.h, b3["mmav"] + (hp - 1) * 256)
                            sy.wait_ge(sqs.h, b3["sqs"] + (hp - 1) * 5)
                        if hp == 0:
                            ccv.wait(sy)
                        ldh[par].inc(sy.dma_start(
                            qtile[par][:], QD[hp * 128:(hp + 1) * 128, :]), 16)
                        for r in range(2):
                            ldh[par].inc(sy.dma_start(
                                ktile[par][:, r * T:(r + 1) * T],
                                KVOUT[r * 2 * D + hp * 128: r * 2 * D + (hp + 1) * 128, :]), 16)
                        for hloc in range(2):
                            for r in range(2):
                                src = KVOUT[r * 2 * D + D: r * 2 * D + 2 * D,
                                            (2 * hp + hloc) * 64:(2 * hp + hloc) * 64 + 64]
                                ldh[par].inc(sy.dma_start(
                                    vbuf[par][hloc].rearrange("p (kt w) -> p kt w", w=65)
                                    [:, r * 16:(r + 1) * 16, 0:64],
                                    src.rearrange("(tt p) d -> p tt d", p=128)), 16)
                        if hp >= 1:
                            stores(hp - 1)
                    stores(NTT - 1)
                    ctm.wait(sy)

                @blk.vector
                def _(v):
                    inst = None
                    for par in range(2):
                        for h in range(2):
                            inst = v.memset(
                                vbuf[par][h].rearrange("p (kt w) -> p kt w", w=65)[:, :, 64:65],
                                1.0)
                    von.inc(inst)
                    for hp in range(NTT):
                        for r, (hloc, sqi, colb) in enumerate(ROUNDS):
                            v.wait_ge(ms3.h, b3["ms3"] + hp * 7 + r + 1)
                            inst = None
                            for tc in range(4):
                                inst = v.reduce_max(mstat[0:1, colb + tc:colb + tc + 1],
                                                    st_ps[tc], AX.X)
                            dvs.inc(inst)
                        dvs.wait(v)
                        inst = None
                        for hloc in range(2):
                            qb = hloc * 12
                            v.reduce_max(mstat[0:1, 24 + 2 * hloc:25 + 2 * hloc],
                                         mstat[0:1, qb:qb + 4], AX.X)
                            inst = v.reduce_max(mstat[0:1, 25 + 2 * hloc:26 + 2 * hloc],
                                                mstat[0:1, qb + 4:qb + 12], AX.X)
                        dvs.inc(inst)
                        v.wait_ge(sqs.h, b3["sqs"] + hp * 5 + 4)
                        v.tensor_tensor(mstat[0:1, 32:33], mstat[0:1, 28:29],
                                        mstat[0:1, 29:30], ALU.add)
                        dvs.inc(v.tensor_tensor(mstat[0:1, 33:34], mstat[0:1, 30:31],
                                                mstat[0:1, 31:32], ALU.add))
                        v.wait_ge(sqs.h, b3["sqs"] + hp * 5 + 5)
                        dvs.inc(v.tensor_scalar(mstat[0:1, 36:38], mstat[0:1, 34:36],
                                                -1.0 / 8.0, None, ALU.mult))
                        for hloc in range(2):
                            for qc in range(4):
                                g_ = grp(hp, hloc, qc)
                                for pr in range(16):
                                    idx0 = widx(hp, hloc, qc, 2 * pr)
                                    pidx = idx0 // 2
                                    v.wait_ge(ac3.h, b3["ac3"] + pidx + 1)
                                    slot = pidx % 3
                                    j0 = MOFF - (2 * pr) * 128 + qc * 512
                                    e_ap = bass.AP(M, MW - j0, [[MW + 1, 128], [128, 2], [-1, 512]])
                                    pslot = probs[:, slot * 1024:(slot + 1) * 1024]                                        .rearrange("p (j q) -> p j q", q=512)
                                    dv3.inc(v.tensor_tensor(pslot, pslot, e_ap, ALU.mult))
                                v.wait_ge(mmav.h, b3["mmav"] + (g_ + 1) * 32)
                                if g_ >= 1:
                                    v.wait_ge(bcm.h, b3["bcm"] + g_)
                                rep.inc(v.reciprocal(rrow[:], av_ps[64:65, :]))
                                v.wait_ge(bcm.h, b3["bcm"] + g_ + 1)
                                v.wait_ge(cte.h, b3["cte"] + g_ + 1)
                                if hloc == 0 and qc == 0:
                                    v.wait_ge(ctm.h, b3["ctm"] + hp * 64)
                                epc.inc(v.tensor_tensor(
                                    ctx_st[qc][64 * hloc:64 * hloc + 64, :],
                                    ctxr[:], bc_ps[:], ALU.mult))

                @blk.scalar
                def _(s):
                    for hp in range(NTT):
                        par = hp % 2
                        s.wait_ge(ldh[par].h, b_ldh[par] + 112 * (hp // 2 + 1))
                        if hp >= 1:
                            s.wait_ge(ms3.h, b3["ms3"] + hp * 7 - 1)
                        sqs.inc(s.activation(sqt[:], qtile[par][:], AF.Square))
                        s.wait_ge(ms3.h, b3["ms3"] + hp * 7 + 2)
                        sqs.inc(s.activation(sqt[:], ktile[par][:, 0:T], AF.Square))
                        s.wait_ge(ms3.h, b3["ms3"] + hp * 7 + 4)
                        sqs.inc(s.activation(sqt[:], ktile[par][:, T:S], AF.Square))
                        s.wait_ge(dvs.h, b3["dvs"] + hp * 9 + 7)
                        sqs.inc(s.activation(mstat[0:1, 28:32], mstat[0:1, 24:28], AF.Ln))
                        s.wait_ge(dvs.h, b3["dvs"] + hp * 9 + 8)
                        sqs.inc(s.activation(mstat[0:1, 34:36], mstat[0:1, 32:34],
                                             AF.Exp, scale=0.5))
                        s.wait_ge(ms3.h, b3["ms3"] + hp * 7 + 7)
                        if hp >= 1:
                            s.wait_ge(ac3.h, b3["ac3"] + hp * 128)
                        bse.inc(s.copy(bias_h[0][:], stb_ps[0]))
                        bse.inc(s.copy(bias_h[1][:], stb_ps[1]))
                        for hloc in range(2):
                            for qc in range(4):
                                g_ = grp(hp, hloc, qc)
                                for pr in range(16):
                                    idx0 = widx(hp, hloc, qc, 2 * pr)
                                    pidx = idx0 // 2
                                    s.wait_ge(mmq.h, b3["mmq"] + idx0 + 2)
                                    if pidx >= 3:
                                        s.wait_ge(mmav.h, b3["mmav"] + (pidx - 3) * 2 + 2)
                                    slot = pidx % 3
                                    ac3.inc(s.activation(
                                        probs[:, slot * 1024:(slot + 1) * 1024],
                                        qk_ps[pidx % 2][:], AF.Exp, scale=0.125,
                                        bias=bias_h[hloc][:]))
                                s.wait_ge(mmav.h, b3["mmav"] + (g_ + 1) * 32)
                                if g_ >= 1:
                                    s.wait_ge(epc.h, b3["epc"] + g_)
                                cte.inc(s.copy(ctxr[:], av_ps[0:64, :]))

                @blk.tensor
                def _(t):
                    stm.wait(t)
                    von.wait(t)
                    for hp in range(NTT):
                        par = hp % 2
                        for r, (hloc, sqi, colb) in enumerate(ROUNDS):
                            t.wait_ge(sqs.h, b3["sqs"] + hp * 5 + sqi + 1)
                            if r >= 1:
                                t.wait_ge(dvs.h, b3["dvs"] + hp * 9 + r)
                            inst = None
                            for tc in range(4):
                                inst = t.matmul(st_ps[tc],
                                                oheads[:, hloc:hloc + 1],
                                                sqt[:, tc * 512:(tc + 1) * 512],
                                                start=True, stop=True)
                            ms3.inc(inst)
                        t.wait_ge(dvs.h, b3["dvs"] + hp * 9 + 9)
                        if hp >= 1:
                            t.wait_ge(bse.h, b3["bse"] + hp * 2)
                        t.matmul(stb_ps[0], onesPf[:], mstat[0:1, 36:37],
                                 start=True, stop=True)
                        ms3.inc(t.matmul(stb_ps[1], onesPf[:], mstat[0:1, 37:38],
                                         start=True, stop=True))
                        for hloc in range(2):
                            hb = 64 * hloc
                            for qc in range(4):
                                g_ = grp(hp, hloc, qc)

                                def av(kt):
                                    idx = widx(hp, hloc, qc, kt)
                                    pidx = idx // 2
                                    t.wait_ge(dv3.h, b3["dv3"] + pidx + 1)
                                    if kt == 0 and g_ >= 1:
                                        t.wait_ge(rep.h, b3["rep"] + g_)
                                        t.wait_ge(cte.h, b3["cte"] + g_)
                                    mmav.inc(t.matmul(
                                        av_ps[:],
                                        vbuf[par][hloc][:, kt * 65:kt * 65 + 65],
                                        probs[:, (pidx % 3) * 1024 + (kt % 2) * 512:
                                              (pidx % 3) * 1024 + (kt % 2) * 512 + 512],
                                        start=(kt == 0), stop=(kt == 31)))

                                for kt in range(32):
                                    idx = widx(hp, hloc, qc, kt)
                                    pidx = idx // 2
                                    if kt % 2 == 0 and pidx >= 2:
                                        t.wait_ge(ac3.h, b3["ac3"] + pidx - 1)
                                    if hloc == 0 and qc == 0 and kt == 0:
                                        t.wait_ge(bse.h, b3["bse"] + hp * 2 + 2)
                                    mmq.inc(t.matmul(
                                        qk_ps[pidx % 2][:, (kt % 2) * 512:(kt % 2) * 512 + 512],
                                        ktile[par][hb:hb + 64, kt * 128:(kt + 1) * 128],
                                        qtile[par][hb:hb + 64, qc * 512:(qc + 1) * 512],
                                        start=True, stop=True))
                                    if kt >= 1:
                                        av(kt - 1)
                                av(31)
                                t.wait_ge(rep.h, b3["rep"] + g_ + 1)
                                if g_ >= 1:
                                    t.wait_ge(epc.h, b3["epc"] + g_)
                                bcm.inc(t.matmul(bc_ps[:], onesPf[0:1, 0:64], rrow[:],
                                                 start=True, stop=True))

    emit_acore()
    if upto == PHASES.index("acore"):
        _finish(nc, mk, x, tout)
        mk.ctx.close()
        return nc

    def emit_aout():
        swo = [mk.sem("swo0"), mk.sem("swo1")]
        ldc = [mk.sem("ldc0"), mk.sem("ldc1")]
        mmo = mk.sem("mmo")
        dvo = mk.sem("dvo")
        b_mmo, b_dvo = mmo.v, dvo.v
        b_swo = [swo[0].v, swo[1].v]
        b_ldc = [ldc[0].v, ldc[1].v]
        with ExitStack() as pctx:
            sb = lambda n, s, d: pctx.enter_context(nc.sbuf_tensor(n, s, d))
            wbuf = [sb(f"wpo{i}", [128, 16 * 512], BF16) for i in range(2)]
            cbuf = [sb(f"cbo{i}", [128, 16 * 512], BF16) for i in range(2)]
            po = [pctx.enter_context(nc.psum_tensor(f"po4{i}", [128, 512], F32))
                  for i in range(4)]
            with nc.Block("aout") as blk:
                @blk.gpsimd
                def _(g):
                    for oq in range(4):
                        if oq >= 2:
                            g.wait_ge(mmo.h, b_mmo + (oq - 1) * 16)
                        swo[oq % 2].inc(g.dma_start(
                            wbuf[oq % 2][:],
                            woT[:, oq * 512:(oq + 1) * 512]
                            .rearrange("(ic p) o -> p ic o", p=128)), 16)

                @blk.sync
                def _(sy):
                    for oq in range(4):
                        for tc in range(4):
                            li = oq * 4 + tc
                            if li >= 2:
                                sy.wait_ge(mmo.h, b_mmo + (li - 1) * 4)
                            ldc[li % 2].inc(sy.dma_start(
                                cbuf[li % 2][:],
                                CTXD[:, tc * 512:(tc + 1) * 512]
                                .rearrange("(fc p) t -> p fc t", p=128)), 16)

                @blk.tensor
                def _(t):
                    for oq in range(4):
                        t.wait_ge(swo[oq % 2].h, b_swo[oq % 2] + 16 * (oq // 2 + 1))
                        for tc in range(4):
                            li = oq * 4 + tc
                            t.wait_ge(ldc[li % 2].h, b_ldc[li % 2] + 16 * (li // 2 + 1))
                            for oloc in range(4):
                                oc = oq * 4 + oloc
                                gi = li * 4 + oloc
                                bank = gi % 4
                                if gi >= 4:
                                    t.wait_ge(dvo.h, b_dvo + gi - 3)
                                inst = None
                                for fc in range(NIC):
                                    co = fc * 512 + oloc * 128
                                    inst = t.matmul(
                                        po[bank][:], wbuf[oq % 2][:, co:co + 128],
                                        cbuf[li % 2][:, fc * 512:(fc + 1) * 512],
                                        start=(fc == 0), stop=(fc == NIC - 1))
                                mmo.inc(inst)

                @blk.vector
                def _(v):
                    for oq in range(4):
                        for tc in range(4):
                            for oloc in range(4):
                                oc = oq * 4 + oloc
                                gi = (oq * 4 + tc) * 4 + oloc
                                v.wait_ge(mmo.h, b_mmo + gi + 1)
                                sl = slice(tc * 512, (tc + 1) * 512)
                                dvo.inc(v.tensor_tensor(x[oc][:, sl], x[oc][:, sl],
                                                        po[gi % 4][:], ALU.add))

    emit_aout()
    if upto == PHASES.index("aout"):
        _finish(nc, mk, x, tout)
        mk.ctx.close()
        return nc

    emit_ln("2", x, xln)
    if upto == PHASES.index("ln2"):
        _finish(nc, mk, xln, tout)
        mk.ctx.close()
        return nc

    def emit_ffn(tag):
        wsf = [mk.sem("wsf0"), mk.sem("wsf1")]
        w2s = [mk.sem("w2s0"), mk.sem("w2s1")]
        wrs = mk.sem("wrs")
        mmr = mk.sem("mmr")
        gse = mk.sem("gse")
        gsd = mk.sem("gsd")
        acr = mk.sem("acr")
        mmu = mk.sem("mmu")
        mmg = mk.sem("mmg")
        mmo2 = mk.sem("mmo2")
        acS = mk.sem("acS")
        dvT = mk.sem("dvT")
        dvF = mk.sem("dvF")
        dvX = mk.sem("dvX")
        dvB = mk.sem("dvB")
        b = {n: mk.sems[n].v for n in
             ("mmr", "gse", "gsd", "acr", "mmu", "mmg", "mmo2", "acS",
              "dvT", "dvF", "dvX", "dvB", "wrs")}
        b_wsf = [wsf[0].v, wsf[1].v]
        b_w2s = [w2s[0].v, w2s[1].v]
        NH = len(HCH)
        ETCP = [(e, tcp) for e in range(-1, NE) for tcp in range(2)]

        with ExitStack() as pctx:
            sb = lambda n, s, d: pctx.enter_context(nc.sbuf_tensor(n, s, d))
            hbuf = [sb(f"hb{tag}_{i}", [128, 1024], BF16) for i in range(NH)]
            w1b = [sb(f"w1b{tag}{i}", [128, 16 * 128], BF16) for i in range(2)]
            w3b = [sb(f"w3b{tag}{i}", [128, 16 * 128], BF16) for i in range(2)]
            w2b = [sb(f"w2b{tag}{i}", [128, NH * 128], BF16) for i in range(2)]
            wrb = sb(f"wrb{tag}", [128, 16 * 8], BF16)
            sub = [sb(f"su{tag}{i}", [128, 512], BF16) for i in range(2)]
            tmb = [sb(f"tm{tag}{i}", [128, 512], BF16) for i in range(2)]
            bcolb = [sb(f"bcl{tag}{i}", [128, 512], BF16) for i in range(2)]
            gates = sb(f"gt{tag}", [128, 128], F32)
            p8b = sb(f"p8{tag}", [128, 8], F32)
            oh1b = sb(f"oh1{tag}", [128, 8], F32)
            oh2b = sb(f"oh2{tag}", [128, 8], F32)
            p2b = sb(f"p2{tag}", [128, 8], F32)
            tpb = sb(f"tp{tag}", [128, 8], F32)
            m1b = sb(f"m1{tag}", [128, 1], F32)
            m2b = sb(f"m2{tag}", [128, 1], F32)
            nmb = sb(f"nm{tag}", [128, 1], F32)
            deb = sb(f"de{tag}", [128, 1], F32)
            reb = sb(f"re{tag}", [128, 1], F32)
            pu = [pctx.enter_context(nc.psum_tensor(f"pu{tag}{i}", [128, 512], F32)) for i in range(2)]
            pv = [pctx.enter_context(nc.psum_tensor(f"pv{tag}{i}", [128, 512], F32)) for i in range(2)]
            pg = [pctx.enter_context(nc.psum_tensor(f"pg{tag}{i}", [128, 512], F32)) for i in range(2)]
            po = [pctx.enter_context(nc.psum_tensor(f"po{tag}{i}", [128, 512], F32)) for i in range(2)]

            def w13src(e, hb):
                if e < 0:
                    return (sw1R[hb], sw3R[hb])
                return (ew1R[e, hb], ew3R[e, hb])

            def w2src(e, oc):
                return sw2R[oc] if e < 0 else ew2R[e, oc]

            with nc.Block(f"ffn{tag}") as blk:
                @blk.gpsimd
                def _(g):
                    wrs.inc(g.dma_start(
                        wrb.rearrange("p (ic w) -> p ic w", w=8),
                        wrTp[:, 0:8].rearrange("(ic p) w -> p ic w", p=128)), 16)
                    for ei_, (e, tcp) in enumerate(ETCP):
                        for hb, (hb0, hn) in enumerate(HCH):
                            lb = ei_ * NH + hb
                            par = lb % 2
                            if lb >= 2:
                                g.wait_ge(mmu.h, b["mmu"] + (lb - 1) * 4)
                            s1, s3 = w13src(e, hb)
                            wsf[par].inc(g.dma_start(w1b[par][:], s1), 16)
                            wsf[par].inc(g.dma_start(w3b[par][:], s3), 16)
                        for oc in range(16):
                            ob = ei_ * 16 + oc
                            par = ob % 2
                            if ob >= 2:
                                g.wait_ge(mmo2.h, b["mmo2"] + (ob - 1) * 2)
                            w2s[par].inc(g.dma_start(w2b[par][:], w2src(e, oc)), 16)

                @blk.tensor
                def _(t):
                    wrs.wait(t)
                    for tt in range(16):
                        if tt >= 2:
                            t.wait_ge(gse.h, b["gse"] + tt - 1)
                        inst = None
                        for ic in range(NIC):
                            inst = t.matmul(po[tt % 2][:, 0:8],
                                            xln[ic][:, tt * 128:(tt + 1) * 128],
                                            wrb[:, ic * 8:ic * 8 + 8],
                                            start=(ic == 0), stop=(ic == NIC - 1))
                        mmr.inc(inst)
                    for ei_, (e, tcp) in enumerate(ETCP):
                        if e >= 0:
                            if ei_ >= 1:
                                t.wait_ge(dvF.h, b["dvF"] + ei_ * 44)
                            for t5loc in range(2):
                                gix = (ei_ - 2) * 2 + t5loc
                                inst = None
                                for ttl in range(4):
                                    bix = (ei_ - 2) * 8 + t5loc * 4 + ttl
                                    t.wait_ge(dvB.h, b["dvB"] + bix + 1)
                                    inst = t.matmul(pg[t5loc][:, ttl * 128:(ttl + 1) * 128],
                                                    bcolb[t5loc][:, ttl * 128:(ttl + 1) * 128],
                                                    i128[:], start=True, stop=True)
                                mmg.inc(inst)
                        for hb, (hb0, hn) in enumerate(HCH):
                            lb = ei_ * NH + hb
                            par = lb % 2
                            t.wait_ge(wsf[par].h, b_wsf[par] + 32 * (lb // 2 + 1))
                            for t5loc in range(2):
                                t5 = tcp * 2 + t5loc
                                gu = ei_ * NH * 4 + hb * 4 + t5loc * 2
                                sidx = ei_ * 44 + hb * 2 + t5loc
                                if sidx >= 2:
                                    t.wait_ge(acS.h, b["acS"] + sidx - 1)
                                    t.wait_ge(dvT.h, b["dvT"] + sidx - 1)
                                inst = None
                                for ic in range(NIC):
                                    inst = t.matmul(pu[t5loc][0:hn, :],
                                                    w1b[par][:, ic * 128:ic * 128 + hn],
                                                    xln[ic][:, t5 * 512:(t5 + 1) * 512],
                                                    start=(ic == 0), stop=(ic == NIC - 1))
                                mmu.inc(inst)
                                inst = None
                                for ic in range(NIC):
                                    inst = t.matmul(pv[t5loc][0:hn, :],
                                                    w3b[par][:, ic * 128:ic * 128 + hn],
                                                    xln[ic][:, t5 * 512:(t5 + 1) * 512],
                                                    start=(ic == 0), stop=(ic == NIC - 1))
                                mmu.inc(inst)
                        for oc in range(16):
                            ob = ei_ * 16 + oc
                            par = ob % 2
                            t.wait_ge(w2s[par].h, b_w2s[par] + 16 * (ob // 2 + 1))
                            if oc == 0:
                                t.wait_ge(dvF.h, b["dvF"] + (ei_ + 1) * 44)
                                if ei_ == 0:
                                    t.wait_ge(gse.h, b["gse"] + 16)
                            for t5loc in range(2):
                                og = ei_ * 32 + oc * 2 + t5loc
                                if og >= 2:
                                    t.wait_ge(dvX.h, b["dvX"] + og - 1)
                                inst = None
                                for hc, (hb0, hn) in enumerate(HCH):
                                    inst = t.matmul(po[og % 2][:],
                                                    w2b[par][0:hn, hc * 128:(hc + 1) * 128],
                                                    hbuf[hc][0:hn, t5loc * 512:(t5loc + 1) * 512],
                                                    start=(hc == 0), stop=(hc == NH - 1))
                                mmo2.inc(inst)

                @blk.scalar
                def _(s):
                    for tt in range(16):
                        s.wait_ge(mmr.h, b["mmr"] + tt + 1)
                        s.wait_ge(gsd.h, b["gsd"] + tt * 11 + 1)
                        acr.inc(s.activation(p8b[:], po[tt % 2][:, 0:8], AF.Exp,
                                             bias=nmb[:]))
                    for ei_, (e, tcp) in enumerate(ETCP):
                        for hb, (hb0, hn) in enumerate(HCH):
                            for t5loc in range(2):
                                gu = ei_ * NH * 4 + hb * 4 + t5loc * 2
                                sidx = ei_ * 44 + hb * 2 + t5loc
                                s.wait_ge(mmu.h, b["mmu"] + gu + 1)
                                if sidx >= 2:
                                    s.wait_ge(dvT.h, b["dvT"] + sidx - 1)
                                acS.inc(s.activation(sub[t5loc][0:hn, :],
                                                     pu[t5loc][0:hn, :], AF.Silu))

                @blk.vector
                def _(v):
                    for tt in range(16):
                        c0 = b["gsd"] + tt * 11
                        v.wait_ge(mmr.h, b["mmr"] + tt + 1)
                        if tt >= 1:
                            v.wait_ge(gse.h, b["gse"] + tt)
                            v.wait_ge(acr.h, b["acr"] + tt)
                        gsd.inc(v.reduce_max(nmb[:], po[tt % 2][:, 0:8], AX.X, negate=True))
                        v.wait_ge(acr.h, b["acr"] + tt + 1)
                        gsd.inc(v.reduce_max(m1b[:], p8b[:], AX.X))
                        v.wait_ge(gsd.h, c0 + 2)
                        gsd.inc(v.tensor_scalar(oh1b[:], p8b[:], m1b[:], None, ALU.is_ge))
                        v.wait_ge(gsd.h, c0 + 3)
                        gsd.inc(v.tensor_mul(tpb[:], p8b[:], oh1b[:]))
                        v.wait_ge(gsd.h, c0 + 4)
                        gsd.inc(v.tensor_sub(p2b[:], p8b[:], tpb[:]))
                        v.wait_ge(gsd.h, c0 + 5)
                        gsd.inc(v.reduce_max(m2b[:], p2b[:], AX.X))
                        v.wait_ge(gsd.h, c0 + 6)
                        gsd.inc(v.tensor_scalar(oh2b[:], p2b[:], m2b[:], None, ALU.is_ge))
                        v.wait_ge(gsd.h, c0 + 7)
                        gsd.inc(v.tensor_add(oh1b[:], oh1b[:], oh2b[:]))
                        gsd.inc(v.tensor_add(deb[:], m1b[:], m2b[:]))
                        v.wait_ge(gsd.h, c0 + 9)
                        gsd.inc(v.reciprocal(reb[:], deb[:]))
                        v.wait_ge(gsd.h, c0 + 8)
                        gsd.inc(v.tensor_mul(tpb[:], p8b[:], oh1b[:]))
                        v.wait_ge(gsd.h, c0 + 11)
                        gse.inc(v.tensor_scalar(gates[:, tt * 8:(tt + 1) * 8], tpb[:],
                                                reb[:], None, ALU.mult))
                    for ei_, (e, tcp) in enumerate(ETCP):
                        if e >= 0:
                            for t5loc in range(2):
                                t5 = tcp * 2 + t5loc
                                for ttl in range(4):
                                    tt = t5 * 4 + ttl
                                    bix = (ei_ - 2) * 8 + t5loc * 4 + ttl
                                    if bix == 0:
                                        v.wait_ge(gse.h, b["gse"] + 16)
                                    if ei_ >= 3:
                                        v.wait_ge(mmg.h, b["mmg"] + (ei_ - 2) * 2)
                                    dvB.inc(v.tensor_scalar(
                                        bcolb[t5loc][:, ttl * 128:(ttl + 1) * 128], o128[:],
                                        gates[:, tt * 8 + e: tt * 8 + e + 1], None, ALU.mult))
                        for hb, (hb0, hn) in enumerate(HCH):
                            for t5loc in range(2):
                                gu = ei_ * NH * 4 + hb * 4 + t5loc * 2
                                sidx = ei_ * 44 + hb * 2 + t5loc
                                v.wait_ge(acS.h, b["acS"] + sidx + 1)
                                v.wait_ge(mmu.h, b["mmu"] + gu + 2)
                                dvT.inc(v.tensor_mul(tmb[t5loc][0:hn, :],
                                                     sub[t5loc][0:hn, :],
                                                     pv[t5loc][0:hn, :]))
                                v.wait_ge(dvT.h, b["dvT"] + sidx + 1)
                                if ei_ >= 1 and hb == 0 and t5loc == 0:
                                    v.wait_ge(mmo2.h, b["mmo2"] + ei_ * 32)
                                if e >= 0:
                                    gix = (ei_ - 2) * 2 + t5loc
                                    v.wait_ge(mmg.h, b["mmg"] + gix + 1)
                                    dvF.inc(v.tensor_mul(
                                        hbuf[hb][0:hn, t5loc * 512:(t5loc + 1) * 512],
                                        tmb[t5loc][0:hn, :], pg[t5loc][0:hn, :]))
                                else:
                                    dvF.inc(v.tensor_copy(
                                        hbuf[hb][0:hn, t5loc * 512:(t5loc + 1) * 512],
                                        tmb[t5loc][0:hn, :]))
                        for oc in range(16):
                            for t5loc in range(2):
                                t5 = tcp * 2 + t5loc
                                og = ei_ * 32 + oc * 2 + t5loc
                                v.wait_ge(mmo2.h, b["mmo2"] + og + 1)
                                sl = slice(t5 * 512, (t5 + 1) * 512)
                                dvX.inc(v.tensor_tensor(x[oc][:, sl], x[oc][:, sl],
                                                        po[og % 2][:], ALU.add))

    emit_ffn("f1")
    if upto == PHASES.index("ffn1"):
        _finish(nc, mk, x, tout)
        mk.ctx.close()
        return nc

    emit_ln("3", x, xln)
    if upto == PHASES.index("ln3"):
        _finish(nc, mk, xln, tout)
        mk.ctx.close()
        return nc

    def emit_bprep():
        swb = [mk.sem("swb0"), mk.sem("swb1")]
        swd = mk.sem("swd")
        mmf = mk.sem("mmf")
        acf = mk.sem("acf")
        dvz = mk.sem("dvz")
        stw = mk.sem("stw")
        stwk = mk.sem("stwk")
        stwo = mk.sem("stwo")
        stwv = mk.sem("stwv")
        swr = [mk.sem("swr0"), mk.sem("swr1")]
        swv = mk.sem("swv")
        mmp = mk.sem("mmp")
        acp = mk.sem("acp")
        stp = [mk.sem("stp0"), mk.sem("stp1")]
        ccb = mk.sem("ccb")
        bb = {n: mk.sems[n].v for n in
              ("swd", "mmf", "acf", "dvz", "stw", "stwk", "stwo", "stwv", "swv", "mmp", "acp", "ccb")}
        b_swb = [swb[0].v, swb[1].v]
        b_swr = [swr[0].v, swr[1].v]
        b_stp = [stp[0].v, stp[1].v]

        with ExitStack() as pctx:
            sb = lambda n, s, d: pctx.enter_context(nc.sbuf_tensor(n, s, d))
            wdn_sb = sb("wdnsb", [128, RK], BF16)
            wup_sb = sb("wupsb", [128, RK], BF16)
            bwb = [sb(f"bwb{i}", [128, D], BF16) for i in range(2)]
            stg = [sb(f"stg8{i}", [128, 1024], BF16) for i in range(16)]
            wofstg = [sb(f"wof8{i}", [128, D], BF16) for i in range(8)]
            pf = [pctx.enter_context(nc.psum_tensor(f"pf8{i}", [128, 512], F32))
                  for i in range(4)]

            with nc.Block("bprep") as blk:
                @blk.gpsimd
                def _(g):
                    swd.inc(g.dma_start(wdn_sb[:], wdn[:]), 16)
                    swd.inc(g.dma_start(wup_sb[:], wupT[:]), 16)
                    for wi, wsrc in enumerate([bWq, bWk, bWoT]):
                        for h in range(NHB):
                            lb = wi * NHB + h
                            par = lb % 2
                            if lb >= 2:
                                g.wait_ge(mmf.h, bb["mmf"] + _fold_cum(lb - 1))
                            row0 = h * DHB
                            swb[par].inc(g.dma_start(
                                bwb[par][:], wsrc[row0:row0 + DHB, :]), 16)

                @blk.vector
                def _(v):
                    inst = None
                    for ic in range(16):
                        inst = v.memset(stg[ic][:], 0.0)
                    for i in range(8):
                        inst = v.memset(wofstg[i][:], 0.0)
                    dvz.inc(inst)

                @blk.tensor
                def _(t):
                    swd.wait(t)
                    for wi in range(3):
                        for h in range(NHB):
                            lb = wi * NHB + h
                            par = lb % 2
                            t.wait_ge(swb[par].h, b_swb[par] + 16 * (lb // 2 + 1))
                            if wi < 2:
                                for ic in range(NIC):
                                    bank = ic % 4
                                    if lb > 0 or ic >= 4:
                                        t.wait_ge(acf.h, bb["acf"] + lb * NIC + ic - 3)
                                    mmf.inc(t.matmul(pf[bank][0:128, 0:RK],
                                                     bwb[par][:, ic * 128:(ic + 1) * 128],
                                                     wdn_sb[:], start=True, stop=True))
                            else:
                                for oc4 in range(4):
                                    bank = oc4 % 4
                                    t.wait_ge(acf.h, bb["acf"] + 2 * NHB * NIC + (lb - 2 * NHB) * 4 + oc4 - 3)
                                    mmf.inc(t.matmul(pf[bank][0:RK, :],
                                                     wup_sb[:],
                                                     bwb[par][:, oc4 * 512:(oc4 + 1) * 512],
                                                     start=True, stop=True))

                @blk.scalar
                def _(s):
                    dvz.wait(s)
                    for wi in range(3):
                        for h in range(NHB):
                            lb = wi * NHB + h
                            if wi < 2:
                                for ic in range(NIC):
                                    s.wait_ge(mmf.h, bb["mmf"] + _fold_cum(lb) + ic + 1)
                                    if wi == 1 and h == 0 and ic == 0:
                                        s.wait_ge(stw.h, bb["stw"] + 16 * 16)
                                    acf.inc(s.copy(stg[ic][:, 64 * h:64 * h + RK],
                                                   pf[ic % 4][0:128, 0:RK]))
                            else:
                                for oc4 in range(4):
                                    s.wait_ge(mmf.h, bb["mmf"] + _fold_cum(lb) + oc4 + 1)
                                    acf.inc(s.copy(
                                        wofstg[h // 2][64 * (h % 2):64 * (h % 2) + RK,
                                                       oc4 * 512:(oc4 + 1) * 512],
                                        pf[oc4 % 4][0:RK, :]))

                @blk.sync
                def _(sy):
                    sy.wait_ge(acf.h, bb["acf"] + NHB * NIC)
                    for ic in range(16):
                        stw.inc(sy.dma_start(
                            WQRD.rearrange("(c p) s -> c p s", p=128)[ic], stg[ic][:]), 16)
                    sy.wait_ge(acf.h, bb["acf"] + 2 * NHB * NIC)
                    for ic in range(16):
                        stwk.inc(sy.dma_start(
                            WKRD.rearrange("(c p) s -> c p s", p=128)[ic], stg[ic][:]), 16)
                    sy.wait_ge(acf.h, bb["acf"] + 2 * NHB * NIC + NHB * 4)
                    for i in range(8):
                        stwo.inc(sy.dma_start(
                            WOFD.rearrange("(c p) o -> c p o", p=128)[i], wofstg[i][:]), 16)

                @blk.gpsimd
                def _(g):
                    for h in range(NHB):
                        lb = 3 * NHB + h
                        par = lb % 2
                        g.wait_ge(mmf.h, bb["mmf"] + _fold_cum(lb - 1))
                        row0 = h * DHB
                        swb[par].inc(g.dma_start(
                            bwb[par][:], bWv[row0:row0 + DHB, :]), 16)

                @blk.tensor
                def _(t):
                    for h in range(NHB):
                        lb = 3 * NHB + h
                        par = lb % 2
                        t.wait_ge(swb[par].h, b_swb[par] + 16 * (lb // 2 + 1))
                        for ic in range(NIC):
                            bank = ic % 4
                            t.wait_ge(acf.h, bb["acf"] + 2 * NHB * NIC + NHB * 4 + (lb - 3 * NHB) * NIC + ic - 3)
                            mmf.inc(t.matmul(pf[bank][0:128, 0:RK],
                                             bwb[par][:, ic * 128:(ic + 1) * 128],
                                             wdn_sb[:], start=True, stop=True))

                @blk.scalar
                def _(s):
                    for h in range(NHB):
                        lb = 3 * NHB + h
                        for ic in range(NIC):
                            s.wait_ge(mmf.h, bb["mmf"] + _fold_cum(lb) + ic + 1)
                            if h == 0 and ic == 0:
                                s.wait_ge(stwk.h, bb["stwk"] + 16 * 16)
                            acf.inc(s.copy(stg[ic][:, 32 * h:32 * h + RK],
                                           pf[ic % 4][0:128, 0:RK]))

                @blk.sync
                def _(sy):
                    sy.wait_ge(acf.h, bb["acf"] + 2 * NHB * NIC + NHB * 4 + NHB * NIC)
                    for ic in range(16):
                        stwv.inc(sy.dma_start(
                            WVRD.rearrange("(c p) s -> c p s", p=128)[ic],
                            stg[ic][:, 0:512]), 16)
                    stw.wait(sy); stwk.wait(sy); stwo.wait(sy); stwv.wait(sy)

        with ExitStack() as pctx:
            sb = lambda n, s, d: pctx.enter_context(nc.sbuf_tensor(n, s, d))
            wsbq = [sb(f"wsbq{i}", [128, 2048], BF16) for i in range(2)]
            krst = [sb(f"krst{i}", [128, 2048], BF16) for i in range(2)]
            wvr_sb = sb("wvrsb", [128, 16 * 512], BF16)
            vstg = [sb(f"vstg8{i}", [128, 528], BF16) for i in range(2)]
            pf = [pctx.enter_context(nc.psum_tensor(f"pf8b{i}", [128, 512], F32))
                  for i in range(4)]

            with nc.Block("bproj") as blk:
                @blk.vector
                def _(v):
                    inst2 = None
                    for i in range(2):
                        inst2 = v.memset(vstg[i][:], 0.0)
                    dvz.inc(inst2)
                    dvz.wait(v)
                    for i in range(2):
                        for h in range(NHB):
                            inst2 = v.memset(vstg[i][0:128, 33 * h + 32:33 * h + 33], 1.0)
                    dvz.inc(inst2)

                @blk.scalar
                def _(s):
                    dvz.wait(s)

                @blk.sync
                def _(sy):
                    stw.wait(sy); stwk.wait(sy); stwo.wait(sy); stwv.wait(sy)

                    def krload(li):
                        src = WQRD if li < 8 else WKRD
                        st = li % 8
                        par = li % 2
                        if li >= 2:
                            sy.wait_ge(mmp.h, bb["mmp"] + (li - 1) * 4)
                        swr[par].inc(sy.dma_start(
                            wsbq[par].rearrange("p (ic s) -> p ic s", s=128),
                            src[:, st * 128:(st + 1) * 128]
                            .rearrange("(ic p) s -> p ic s", p=128)), 16)

                    def krstore(li):
                        dst = QRD if li < 8 else KRIN
                        st = li % 8
                        sy.wait_ge(acp.h, bb["acp"] + li * 4 + 4)
                        stp[li % 2].inc(sy.dma_start(
                            dst.rearrange("(c p) t -> c p t", p=128)[st],
                            krst[li % 2][:]), 16)

                    for li in range(16):
                        krload(li)
                        if li >= 1:
                            krstore(li - 1)
                    krstore(15)
                    swv.inc(sy.dma_start(
                        wvr_sb.rearrange("p (ic s) -> p ic s", s=512),
                        WVRD.rearrange("(ic p) s -> p ic s", p=128)), 16)

                @blk.tensor
                def _(t):
                    for wi in range(2):
                        for st in range(8):
                            li = wi * 8 + st
                            par = li % 2
                            t.wait_ge(swr[par].h, b_swr[par] + 16 * (li // 2 + 1))
                            for tc in range(4):
                                gp = li * 4 + tc
                                bank = gp % 4
                                if gp >= 4:
                                    t.wait_ge(acp.h, bb["acp"] + gp - 3)
                                inst = None
                                for ic in range(NIC):
                                    inst = t.matmul(pf[bank][:],
                                                    wsbq[par][:, ic * 128:(ic + 1) * 128],
                                                    xln[ic][:, tc * 512:(tc + 1) * 512],
                                                    start=(ic == 0), stop=(ic == NIC - 1))
                                mmp.inc(inst)
                    swv.wait(t)
                    for tt in range(16):
                        gp = 64 + tt
                        bank = gp % 4
                        t.wait_ge(acp.h, bb["acp"] + gp - 3)
                        inst = None
                        for ic in range(NIC):
                            inst = t.matmul(pf[bank][:],
                                            xln[ic][:, tt * 128:(tt + 1) * 128],
                                            wvr_sb[:, ic * 512:(ic + 1) * 512],
                                            start=(ic == 0), stop=(ic == NIC - 1))
                        mmp.inc(inst)

                @blk.scalar
                def _(s):
                    for wi in range(2):
                        for st in range(8):
                            li = wi * 8 + st
                            for tc in range(4):
                                gp = li * 4 + tc
                                s.wait_ge(mmp.h, bb["mmp"] + gp + 1)
                                if tc == 0 and li >= 2:
                                    s.wait_ge(stp[li % 2].h, b_stp[li % 2] + 16 * (li // 2))
                                acp.inc(s.copy(krst[li % 2][:, tc * 512:(tc + 1) * 512],
                                               pf[gp % 4][:]))
                    for tt in range(16):
                        gp = 64 + tt
                        s.wait_ge(mmp.h, bb["mmp"] + gp + 1)
                        if tt >= 2:
                            s.wait_ge(stp[tt % 2].h, b_stp[tt % 2] + 16 * (8 + tt // 2))
                        inst = None
                        for h in range(NHB):
                            inst = s.copy(vstg[tt % 2][:, 33 * h:33 * h + RK],
                                          pf[gp % 4][:, 32 * h:32 * h + RK])
                        acp.inc(inst, 1)

                @blk.sync
                def _(sy):
                    for tt in range(16):
                        sy.wait_ge(acp.h, bb["acp"] + 64 + tt + 1)
                        stp[tt % 2].inc(sy.dma_start(
                            VRIN.rearrange("(h tt p) w -> tt p h w", tt=16, p=128)[tt],
                            vstg[tt % 2].rearrange("p (h w) -> p h w", w=33)), 16)

                @blk.gpsimd
                def _(g):
                    stp[0].wait(g)
                    stp[1].wait(g)
                    ccb.inc(g.collective_compute(
                        "AllGather", ALU.bypass,
                        replica_groups=[[0, 1], [2, 3], [4, 5], [6, 7]],
                        ins=[KRIN.ap().opt()], outs=[KROUT.ap().opt()]))
                    g.wait_ge(ccb.h, ccb.v)
                    ccb.inc(g.collective_compute(
                        "AllGather", ALU.bypass,
                        replica_groups=[[0, 1], [2, 3], [4, 5], [6, 7]],
                        ins=[VRIN.ap().opt()], outs=[VROUT.ap().opt()]))

    emit_bprep()
    if upto == PHASES.index("bprep"):
        _finish(nc, mk, x, tout)
        mk.ctx.close()
        return nc

    def emit_bcore():
        ldh = [mk.sem("ldh0"), mk.sem("ldh1")]
        sqs = mk.sem("sqs")
        bse = mk.sem("bse")
        cte = mk.sem("cte")
        ms3 = mk.sem("ms3")
        dvs = mk.sem("dvs")
        mmq = mk.sem("mmq")
        mmav = mk.sem("mmav")
        ac3 = mk.sem("ac3")
        rep = mk.sem("rep")
        bcm = mk.sem("bcm")
        epc = mk.sem("epc")
        ctm = mk.sem("ctm")
        dvz2 = mk.sem("dvz2")
        ccb = mk.sems["ccb"]
        RING = 6
        NHP = 8
        BSC = 0.17677669529663687
        b3 = {n: mk.sems[n].v for n in
              ("sqs", "bse", "cte", "ms3", "dvs", "mmq", "mmav", "ac3",
               "rep", "bcm", "epc", "ctm")}
        b_ldh = [ldh[0].v, ldh[1].v]

        with ExitStack() as pctx:
            sb = lambda n, s, d: pctx.enter_context(nc.sbuf_tensor(n, s, d))
            ktile = [sb(f"kt9{i}", [128, S], BF16) for i in range(2)]
            qtile = [sb(f"qt9{i}", [128, T], BF16) for i in range(2)]
            vbuf = [[sb(f"vb9{i}{h}", [128, 32 * 33], BF16) for h in range(2)]
                    for i in range(2)]
            sqt = sb("sqt9", [128, T], BF16)
            probs = sb("probs9", [128, 3 * 1024], BF16)
            ctxr = sb("ctxr9", [32, 512], BF16)
            ctx_st = [sb(f"cst9{i}", [128, 512], BF16) for i in range(4)]
            mstat = sb("mstat9", [1, 40], F32)
            rrow = sb("rrow9", [1, 512], F32)
            bias_h = [sb(f"bh9{i}", [128, 1], F32) for i in range(2)]
            qk_ps = [pctx.enter_context(nc.psum_tensor(f"qk9{i}", [128, 1024], F32))
                     for i in range(2)]
            av_ps = pctx.enter_context(nc.psum_tensor("av9", [33, 512], F32))
            bc_ps = pctx.enter_context(nc.psum_tensor("bc9", [32, 512], F32))
            st_ps = [qk_ps[0][0:1, 0:512], qk_ps[0][0:1, 512:1024],
                     qk_ps[1][0:1, 0:512], qk_ps[1][0:1, 512:1024]]
            stb_ps = [qk_ps[0][0:128, 0:1], qk_ps[1][0:128, 0:1]]

            ROUNDS = [(0, 0, 0), (1, 0, 12), (0, 1, 4), (1, 1, 16), (0, 2, 8), (1, 2, 20)]

            def widx(hp, hloc, qc, kt=0):
                return hp * 256 + (hloc * 4 + qc) * 32 + kt

            def grp(hp, hloc, qc):
                return hp * 8 + hloc * 4 + qc

            with nc.Block("bcore") as blk:
                @blk.sync
                def _(sy):
                    def stores(hp):
                        sy.wait_ge(epc.h, b3["epc"] + hp * 8 + 8)
                        for qc in range(4):
                            ctm.inc(sy.dma_start(
                                CTXD[hp * 128:(hp + 1) * 128, qc * 512:(qc + 1) * 512],
                                ctx_st[qc][:]), 16)

                    for hp in range(NHP):
                        par = hp % 2
                        if hp >= 2:
                            sy.wait_ge(mmav.h, b3["mmav"] + (hp - 1) * 256)
                            sy.wait_ge(sqs.h, b3["sqs"] + (hp - 1) * 5)
                        if hp == 0:
                            ccb.wait(sy)
                        ldh[par].inc(sy.dma_start(
                            qtile[par][:], QRD[hp * 128:(hp + 1) * 128, :]), 16)
                        for r in range(2):
                            ldh[par].inc(sy.dma_start(
                                ktile[par][:, r * T:(r + 1) * T],
                                KROUT[r * 1024 + hp * 128: r * 1024 + (hp + 1) * 128, :]), 16)
                        for hloc in range(2):
                            for r in range(2):
                                src = VROUT.rearrange("(r h tt p) w -> r h p tt w",
                                                      r=2, h=NHB, p=128)[r, 2 * hp + hloc]
                                ldh[par].inc(sy.dma_start(
                                    vbuf[par][hloc].rearrange("p (kt w) -> p kt w", w=33)
                                    [:, r * 16:(r + 1) * 16, :],
                                    src), 16)
                        if hp >= 1:
                            stores(hp - 1)
                    stores(NHP - 1)
                    ctm.wait(sy)

                @blk.vector
                def _(v):
                    inst = None
                    for qc in range(4):
                        inst = v.memset(ctx_st[qc][32:64, :], 0.0)
                        inst = v.memset(ctx_st[qc][96:128, :], 0.0)
                    dvz2.inc(inst)
                    for hp in range(NHP):
                        for r, (hloc, sqi, colb) in enumerate(ROUNDS):
                            v.wait_ge(ms3.h, b3["ms3"] + hp * 7 + r + 1)
                            inst = None
                            for tc in range(4):
                                inst = v.reduce_max(mstat[0:1, colb + tc:colb + tc + 1],
                                                    st_ps[tc], AX.X)
                            dvs.inc(inst)
                        dvs.wait(v)
                        inst = None
                        for hloc in range(2):
                            qb = hloc * 12
                            v.reduce_max(mstat[0:1, 24 + 2 * hloc:25 + 2 * hloc],
                                         mstat[0:1, qb:qb + 4], AX.X)
                            inst = v.reduce_max(mstat[0:1, 25 + 2 * hloc:26 + 2 * hloc],
                                                mstat[0:1, qb + 4:qb + 12], AX.X)
                        dvs.inc(inst)
                        v.wait_ge(sqs.h, b3["sqs"] + hp * 5 + 4)
                        v.tensor_tensor(mstat[0:1, 32:33], mstat[0:1, 28:29],
                                        mstat[0:1, 29:30], ALU.add)
                        dvs.inc(v.tensor_tensor(mstat[0:1, 33:34], mstat[0:1, 30:31],
                                                mstat[0:1, 31:32], ALU.add))
                        v.wait_ge(sqs.h, b3["sqs"] + hp * 5 + 5)
                        dvs.inc(v.tensor_scalar(mstat[0:1, 36:38], mstat[0:1, 34:36],
                                                -BSC, None, ALU.mult))
                        for hloc in range(2):
                            for qc in range(4):
                                g_ = grp(hp, hloc, qc)
                                v.wait_ge(mmav.h, b3["mmav"] + (g_ + 1) * 32)
                                if g_ >= 1:
                                    v.wait_ge(bcm.h, b3["bcm"] + g_)
                                rep.inc(v.reciprocal(rrow[:], av_ps[32:33, :]))
                                v.wait_ge(bcm.h, b3["bcm"] + g_ + 1)
                                v.wait_ge(cte.h, b3["cte"] + g_ + 1)
                                if hloc == 0 and qc == 0:
                                    v.wait_ge(ctm.h, b3["ctm"] + hp * 64)
                                epc.inc(v.tensor_tensor(
                                    ctx_st[qc][64 * hloc:64 * hloc + 32, :],
                                    ctxr[:], bc_ps[:], ALU.mult))

                @blk.scalar
                def _(s):
                    for hp in range(NHP):
                        par = hp % 2
                        s.wait_ge(ldh[par].h, b_ldh[par] + 112 * (hp // 2 + 1))
                        if hp >= 1:
                            s.wait_ge(ms3.h, b3["ms3"] + hp * 7 - 1)
                        sqs.inc(s.activation(sqt[:], qtile[par][:], AF.Square))
                        s.wait_ge(ms3.h, b3["ms3"] + hp * 7 + 2)
                        sqs.inc(s.activation(sqt[:], ktile[par][:, 0:T], AF.Square))
                        s.wait_ge(ms3.h, b3["ms3"] + hp * 7 + 4)
                        sqs.inc(s.activation(sqt[:], ktile[par][:, T:S], AF.Square))
                        s.wait_ge(dvs.h, b3["dvs"] + hp * 9 + 7)
                        sqs.inc(s.activation(mstat[0:1, 28:32], mstat[0:1, 24:28], AF.Ln))
                        s.wait_ge(dvs.h, b3["dvs"] + hp * 9 + 8)
                        sqs.inc(s.activation(mstat[0:1, 34:36], mstat[0:1, 32:34],
                                             AF.Exp, scale=0.5))
                        s.wait_ge(ms3.h, b3["ms3"] + hp * 7 + 7)
                        if hp >= 1:
                            s.wait_ge(ac3.h, b3["ac3"] + hp * 128)
                        bse.inc(s.copy(bias_h[0][:], stb_ps[0]))
                        bse.inc(s.copy(bias_h[1][:], stb_ps[1]))
                        for hloc in range(2):
                            for qc in range(4):
                                g_ = grp(hp, hloc, qc)
                                for pr in range(16):
                                    idx0 = widx(hp, hloc, qc, 2 * pr)
                                    pidx = idx0 // 2
                                    s.wait_ge(mmq.h, b3["mmq"] + idx0 + 2)
                                    if pidx >= 3:
                                        s.wait_ge(mmav.h, b3["mmav"] + (pidx - 3) * 2 + 2)
                                    slot = pidx % 3
                                    ac3.inc(s.activation(
                                        probs[:, slot * 1024:(slot + 1) * 1024],
                                        qk_ps[pidx % 2][:], AF.Exp, scale=BSC,
                                        bias=bias_h[hloc][:]))
                                s.wait_ge(mmav.h, b3["mmav"] + (g_ + 1) * 32)
                                if g_ >= 1:
                                    s.wait_ge(epc.h, b3["epc"] + g_)
                                cte.inc(s.copy(ctxr[:], av_ps[0:32, :]))

                @blk.tensor
                def _(t):
                    dvz2.wait(t)
                    for hp in range(NHP):
                        par = hp % 2
                        for r, (hloc, sqi, colb) in enumerate(ROUNDS):
                            t.wait_ge(sqs.h, b3["sqs"] + hp * 5 + sqi + 1)
                            if r >= 1:
                                t.wait_ge(dvs.h, b3["dvs"] + hp * 9 + r)
                            inst = None
                            for tc in range(4):
                                inst = t.matmul(st_ps[tc],
                                                oheads[:, hloc:hloc + 1],
                                                sqt[:, tc * 512:(tc + 1) * 512],
                                                start=True, stop=True)
                            ms3.inc(inst)
                        t.wait_ge(dvs.h, b3["dvs"] + hp * 9 + 9)
                        if hp >= 1:
                            t.wait_ge(bse.h, b3["bse"] + hp * 2)
                        t.matmul(stb_ps[0], onesPf[:], mstat[0:1, 36:37],
                                 start=True, stop=True)
                        ms3.inc(t.matmul(stb_ps[1], onesPf[:], mstat[0:1, 37:38],
                                         start=True, stop=True))
                        for hloc in range(2):
                            hb = 64 * hloc
                            for qc in range(4):
                                g_ = grp(hp, hloc, qc)

                                def av(kt):
                                    idx = widx(hp, hloc, qc, kt)
                                    pidx = idx // 2
                                    t.wait_ge(ac3.h, b3["ac3"] + pidx + 1)
                                    if kt == 0 and g_ >= 1:
                                        t.wait_ge(rep.h, b3["rep"] + g_)
                                        t.wait_ge(cte.h, b3["cte"] + g_)
                                    mmav.inc(t.matmul(
                                        av_ps[:],
                                        vbuf[par][hloc][:, kt * 33:kt * 33 + 33],
                                        probs[:, (pidx % 3) * 1024 + (kt % 2) * 512:
                                              (pidx % 3) * 1024 + (kt % 2) * 512 + 512],
                                        start=(kt == 0), stop=(kt == 31)))

                                for kt in range(32):
                                    idx = widx(hp, hloc, qc, kt)
                                    pidx = idx // 2
                                    if kt % 2 == 0 and pidx >= 2:
                                        t.wait_ge(ac3.h, b3["ac3"] + pidx - 1)
                                    if hloc == 0 and qc == 0 and kt == 0:
                                        t.wait_ge(bse.h, b3["bse"] + hp * 2 + 2)
                                    mmq.inc(t.matmul(
                                        qk_ps[pidx % 2][:, (kt % 2) * 512:(kt % 2) * 512 + 512],
                                        ktile[par][hb:hb + 64, kt * 128:(kt + 1) * 128],
                                        qtile[par][hb:hb + 64, qc * 512:(qc + 1) * 512],
                                        start=True, stop=True))
                                    if kt >= 1:
                                        av(kt - 1)
                                av(31)
                                t.wait_ge(rep.h, b3["rep"] + g_ + 1)
                                if g_ >= 1:
                                    t.wait_ge(epc.h, b3["epc"] + g_)
                                bcm.inc(t.matmul(bc_ps[:], onesPf[0:1, 0:32], rrow[:],
                                                 start=True, stop=True))

    emit_bcore()
    if upto == PHASES.index("bcore"):
        _finish(nc, mk, x, tout)
        mk.ctx.close()
        return nc

    def emit_bout():
        ldw = [mk.sem("ldw0"), mk.sem("ldw1")]
        ldc = [mk.sem("ldc0"), mk.sem("ldc1")]
        mmo = mk.sem("mmo")
        dvo = mk.sem("dvo")
        b_mmo, b_dvo = mmo.v, dvo.v
        b_ldw = [ldw[0].v, ldw[1].v]
        b_ldc = [ldc[0].v, ldc[1].v]
        with ExitStack() as pctx:
            sb = lambda n, s, d: pctx.enter_context(nc.sbuf_tensor(n, s, d))
            wob = [sb(f"wob{i}", [128, 8 * 128], BF16) for i in range(2)]
            cbuf = [sb(f"cbb{i}", [128, 8 * 512], BF16) for i in range(2)]
            po = [pctx.enter_context(nc.psum_tensor(f"pob{i}", [128, 512], F32))
                  for i in range(4)]
            with nc.Block("bout") as blk:
                @blk.sync
                def _(sy):
                    for tc in range(4):
                        if tc >= 2:
                            sy.wait_ge(mmo.h, b_mmo + (tc - 1) * 16)
                        ldc[tc % 2].inc(sy.dma_start(
                            cbuf[tc % 2][:],
                            CTXD[0:1024, tc * 512:(tc + 1) * 512]
                            .rearrange("(sc p) t -> p sc t", p=128)), 16)
                        for oc in range(16):
                            li = tc * 16 + oc
                            if li >= 2:
                                sy.wait_ge(mmo.h, b_mmo + li - 1)
                            ldw[li % 2].inc(sy.dma_start(
                                wob[li % 2].rearrange("p (sc o) -> p sc o", o=128),
                                WOFD[:, oc * 128:(oc + 1) * 128]
                                .rearrange("(sc p) o -> p sc o", p=128)), 16)

                @blk.tensor
                def _(t):
                    for tc in range(4):
                        t.wait_ge(ldc[tc % 2].h, b_ldc[tc % 2] + 16 * (tc // 2 + 1))
                        for oc in range(16):
                            li = tc * 16 + oc
                            t.wait_ge(ldw[li % 2].h, b_ldw[li % 2] + 16 * (li // 2 + 1))
                            if li >= 4:
                                t.wait_ge(dvo.h, b_dvo + li - 3)
                            inst = None
                            for sc in range(8):
                                inst = t.matmul(po[li % 4][:],
                                                wob[li % 2][:, sc * 128:(sc + 1) * 128],
                                                cbuf[tc % 2][:, sc * 512:(sc + 1) * 512],
                                                start=(sc == 0), stop=(sc == 7))
                            mmo.inc(inst)

                @blk.vector
                def _(v):
                    for tc in range(4):
                        for oc in range(16):
                            li = tc * 16 + oc
                            v.wait_ge(mmo.h, b_mmo + li + 1)
                            sl = slice(tc * 512, (tc + 1) * 512)
                            dvo.inc(v.tensor_tensor(x[oc][:, sl], x[oc][:, sl],
                                                    po[li % 4][:], ALU.add))

    emit_bout()
    if upto == PHASES.index("bout"):
        _finish(nc, mk, x, tout)
        mk.ctx.close()
        return nc

    emit_ln("4", x, xln)
    if upto == PHASES.index("ln4"):
        _finish(nc, mk, xln, tout)
        mk.ctx.close()
        return nc

    emit_ffn("f2")

    _finish(nc, mk, x, tout)
    mk.ctx.close()
    return nc


def _finish(nc, mk, src, tout):
    stf = [mk.sem("stf0"), mk.sem("stf1")]
    evf = mk.sem("evf")
    b_evf = evf.v
    with ExitStack() as pctx:
        xf = [pctx.enter_context(nc.sbuf_tensor(f"xf{i}", [128, T], F32)) for i in range(2)]
        with nc.Block("finish") as blk:
            @blk.scalar
            def _(s):
                for i in range(NIC):
                    if i >= 2:
                        s.wait_ge(stf[i % 2].h, 16 * (i // 2))
                    evf.inc(s.copy(xf[i % 2][:], src[i][:]))

            @blk.sync
            def _(sy):
                for i in range(NIC):
                    sy.wait_ge(evf.h, b_evf + i + 1)
                    stf[i % 2].inc(sy.dma_start(
                        tout.rearrange("(c p) t -> c p t", p=128)[i], xf[i % 2][:]), 16)
                stf[0].wait(sy)
                stf[1].wait(sy)


_PATCHED = [False]


def _patch_fast_pjrt():
    if _PATCHED[0]:
        return
    import jax
    from jax.sharding import Mesh, PartitionSpec, NamedSharding
    from jax.experimental.shard_map import shard_map
    import concourse.bass2jax as b2j
    from concourse.bass2jax import _bass_exec_p, install_neuronx_cc_hook, partition_id_tensor

    def fast_run(nc, in_maps, n_cores):
        install_neuronx_cc_hook()
        pn = nc.partition_id_tensor.name if nc.partition_id_tensor else None
        in_names, out_names, out_avals = [], [], []
        zero_shapes = []
        for alloc in nc.m.functions[0].allocations:
            if not isinstance(alloc, mybir.MemoryLocationSet):
                continue
            name = alloc.memorylocations[0].name
            if alloc.kind == "ExternalInput":
                if name != pn:
                    in_names.append(name)
            elif alloc.kind == "ExternalOutput":
                out_names.append(name)
                shape = tuple(alloc.tensor_shape)
                dtype = mybir.dt.np(alloc.dtype)
                out_avals.append(jax.core.ShapedArray(shape, dtype))
                zero_shapes.append((shape, dtype))
        n_params = len(in_names)
        n_outs = len(out_avals)
        all_names = list(in_names) + list(out_names)
        if pn:
            all_names.append(pn)

        def _body(*args):
            operands = list(args)
            if pn:
                operands.append(partition_id_tensor())
            outs = _bass_exec_p.bind(
                *operands, out_avals=tuple(out_avals), in_names=tuple(all_names),
                out_names=tuple(out_names), lowering_input_output_aliases=(),
                sim_require_finite=True, sim_require_nnan=True, nc=nc)
            return tuple(outs)

        devices = jax.devices()[:n_cores]
        donate = tuple(range(n_params, n_params + n_outs))
        if n_cores == 1:
            args = [np.ascontiguousarray(in_maps[0][n]) for n in in_names]
            args += [np.zeros(s, d) for s, d in zero_shapes]
            dargs = [jax.device_put(a, devices[0]) for a in args]
            for a in dargs:
                a.block_until_ready()
            out_arrs = jax.jit(_body, donate_argnums=donate, keep_unused=True)(*dargs)
            return [{n: np.asarray(out_arrs[i]) for i, n in enumerate(out_names)}]

        mesh = Mesh(np.asarray(devices), ("core",))
        sharding = NamedSharding(mesh, PartitionSpec("core"))
        in_specs = (PartitionSpec("core"),) * (n_params + n_outs)
        out_specs = (PartitionSpec("core"),) * n_outs
        sharded = jax.jit(
            shard_map(_body, mesh=mesh, in_specs=in_specs, out_specs=out_specs,
                      check_rep=False),
            donate_argnums=donate, keep_unused=True)
        global_in = []
        for n in in_names:
            shards = [jax.device_put(np.ascontiguousarray(in_maps[c][n]), devices[c])
                      for c in range(n_cores)]
            gshape = (n_cores * shards[0].shape[0],) + tuple(shards[0].shape[1:])
            global_in.append(jax.make_array_from_single_device_arrays(gshape, sharding, shards))
        for s, d in zero_shapes:
            shards = [jax.device_put(np.zeros(s, d), dev) for dev in devices]
            gshape = (n_cores * s[0],) + tuple(s[1:])
            global_in.append(jax.make_array_from_single_device_arrays(gshape, sharding, shards))
        out_arrs = sharded(*global_in)
        res = []
        for c in range(n_cores):
            d = {}
            for i, n in enumerate(out_names):
                a = np.asarray(out_arrs[i]).reshape(n_cores, *out_avals[i].shape)
                d[n] = a[c]
            res.append(d)
        return res

    b2j.run_bass_via_pjrt = fast_run
    _PATCHED[0] = True



def _tile_w13(w):
    z = np.zeros((len(HCH), 128, 16, 128), dtype=np.float32)
    for hb, (hb0, hn) in enumerate(HCH):
        blk = w[hb0:hb0 + hn, :]
        z[hb, :, :, 0:hn] = blk.T.reshape(16, 128, hn).transpose(1, 0, 2)
    return np.ascontiguousarray(z.reshape(len(HCH), 128, 16 * 128))


def _tile_w2(w):
    z = np.zeros((16, 128, len(HCH), 128), dtype=np.float32)
    for hc, (hb0, hn) in enumerate(HCH):
        blk = w[:, hb0:hb0 + hn]
        z[:, 0:hn, hc, :] = blk.T.reshape(hn, 16, 128).transpose(1, 0, 2)
    return np.ascontiguousarray(z.reshape(16, 128, len(HCH) * 128))


def stage_inputs(inputs, declared):
    c = np.ascontiguousarray
    x = np.asarray(inputs["L0_attn_a_hidden_states"])
    common = {
        "wqT": lambda: c(np.asarray(inputs["L0_attn_a_wq"]).T),
        "wkT": lambda: c(np.asarray(inputs["L0_attn_a_wk"]).T),
        "wvT": lambda: c(np.asarray(inputs["L0_attn_a_wv"]).T),
        "woT": lambda: c(np.asarray(inputs["L0_attn_a_wo"]).T),
        "relT": lambda: c(np.asarray(inputs["L0_attn_a_rel_pos"]).T),
        "bWq": lambda: c(np.asarray(inputs["L0_attn_b_Wq"])),
        "bWk": lambda: c(np.asarray(inputs["L0_attn_b_Wk"])),
        "bWv": lambda: c(np.asarray(inputs["L0_attn_b_Wv"])),
        "bWoT": lambda: c(np.asarray(inputs["L0_attn_b_Wo"]).T),
        "wdn": lambda: c(np.asarray(inputs["L0_attn_b_Wdown"])),
        "wupT": lambda: c(np.asarray(inputs["L0_attn_b_Wup"]).T),
        "sw1R": lambda: _tile_w13(np.asarray(inputs["L0_ffn_sw1"])),
        "sw3R": lambda: _tile_w13(np.asarray(inputs["L0_ffn_sw3"])),
        "sw2R": lambda: _tile_w2(np.asarray(inputs["L0_ffn_sw2"])),
        "ew1R": lambda: np.stack([_tile_w13(np.asarray(inputs["L0_ffn_ew1"])[e])
                                  for e in range(NE)]),
        "ew3R": lambda: np.stack([_tile_w13(np.asarray(inputs["L0_ffn_ew3"])[e])
                                  for e in range(NE)]),
        "ew2R": lambda: np.stack([_tile_w2(np.asarray(inputs["L0_ffn_ew2"])[e])
                                  for e in range(NE)]),
        "iden": lambda: np.eye(128, dtype=np.float32),
    }

    def _wrtp():
        w = np.zeros((D, 64), dtype=np.float32)
        w[:, 0:NE] = np.asarray(inputs["L0_ffn_wr"]).T
        return c(w)
    common["wrTp"] = _wrtp
    shared = {k: f() for k, f in common.items() if k in declared}
    in_maps = []
    for core in range(NCORES):
        b, h = core // 2, core % 2
        m = dict(shared)
        if "xin" in declared:
            m["xin"] = c(x[b, h * T:(h + 1) * T, :].T)
        if "halfin" in declared:
            m["halfin"] = np.array([[float(h)]], dtype=np.float32)
        in_maps.append(m)
    return in_maps


def kernel(**inputs):
    _patch_fast_pjrt()
    nc = build_nc(TRUNCATE)
    declared = set()
    for alloc in nc.m.functions[0].allocations:
        if isinstance(alloc, mybir.MemoryLocationSet) and alloc.kind == "ExternalInput":
            declared.add(alloc.memorylocations[0].name)
    in_maps = stage_inputs(inputs, declared)
    res = run_bass_kernel_spmd(nc, in_maps, core_ids=list(range(NCORES)))
    x = np.asarray(inputs["L0_attn_a_hidden_states"])
    out = np.empty((4, S, D), dtype=np.float32)
    for core in range(NCORES):
        b, h = core // 2, core % 2
        out[b, h * T:(h + 1) * T, :] = res.results[core]["tout"].T
    return out


# version 04
# speedup vs baseline: 1.2043x; 1.2043x over previous
import numpy as np
from contextlib import ExitStack

import concourse.bass as bass
import concourse.mybir as mybir
import concourse.bass_utils as _bu
from concourse.bass_utils import run_bass_kernel_spmd

F32 = mybir.dt.float32
U8 = mybir.dt.uint8
BF16 = mybir.dt.bfloat16
AF = mybir.ActivationFunctionType
ALU = mybir.AluOpType
AX = mybir.AxisListType

D = 2048
T = 2048
S = 4096
NCORES = 8
HA, DHA = 32, 64
NHB, DHB, RK = 16, 128, 32
HF = 2730
NE = 8
EPS = 1e-5
HCH = [(128 * i, 128) for i in range(21)] + [(2688, 42)]
NIC = D // 128
NTC = T // 512
NTT = T // 128

PHASES = ["setup", "ln1", "aproj", "acore", "aout", "ln2", "ffn1",
          "ln3", "bprep", "bcore", "bout", "ln4", "ffn2"]
TRUNCATE = None

AW = 6144
MW = 6016
MOFF = 3968


def _fold_cum(lb):
    c = 0
    for j in range(lb):
        c += 4 if 32 <= j < 48 else 16
    return c


class Sem:
    def __init__(self, nc, name):
        self.h = nc.alloc_semaphore(name)
        self.v = 0

    def inc(self, inst, n=1):
        inst.then_inc(self.h, n)
        self.v += n
        return inst

    def wait(self, eng, v=None):
        eng.wait_ge(self.h, self.v if v is None else v)


class MK:
    def __init__(self, nc):
        self.nc = nc
        self.ctx = ExitStack()
        self.sems = {}

    def sem(self, name):
        if name not in self.sems:
            self.sems[name] = Sem(self.nc, name)
        return self.sems[name]


def build_nc(trunc=None):
    nc = bass.Bass("TRN2", target_bir_lowering=False)
    mk = MK(nc)
    upto = PHASES.index(trunc) if trunc else len(PHASES) - 1

    ei = lambda n, s: nc.dram_tensor(n, s, F32, kind="ExternalInput")
    xin = ei("xin", [D, T])
    wqT = ei("wqT", [D, D]); wkT = ei("wkT", [D, D])
    wvT = ei("wvT", [D, D]); woT = ei("woT", [D, D])
    relT = ei("relT", [64, 1023])
    halfin = ei("halfin", [1, 1])
    iden = ei("iden", [128, 128])

    need = lambda name: PHASES.index(name) <= upto
    if need("bprep"):
        bWq = ei("bWq", [D, D]); bWk = ei("bWk", [D, D]); bWv = ei("bWv", [D, D])
        bWoT = ei("bWoT", [D, D])
        wdn = ei("wdn", [DHB, RK])
        wupT = ei("wupT", [DHB, RK])
    if need("ffn1"):
        NH_ = len(HCH)
        sw1R = ei("sw1R", [NH_, 128, 2048]); sw3R = ei("sw3R", [NH_, 128, 2048])
        sw2R = ei("sw2R", [16, 128, NH_ * 128])
        ew1R = ei("ew1R", [NE, NH_, 128, 2048]); ew3R = ei("ew3R", [NE, NH_, 128, 2048])
        ew2R = ei("ew2R", [NE, 16, 128, NH_ * 128])
        wrTp = ei("wrTp", [D, 64])

    tout = nc.dram_tensor("tout", [D, T], F32, kind="ExternalOutput")

    di = lambda n, s, dt=BF16: nc.dram_tensor(n, s, dt, kind="Internal")
    AD = di("AD", [1, AW])
    QD = di("QD", [D, T])
    KVIN = di("KVIN", [2 * D, T]); KVOUT = di("KVOUT", [4 * D, T])
    CTXD = di("CTXD", [D, T])
    if need("bprep"):
        QRD = di("QRD", [1024, T])
        KRIN = di("KRIN", [1024, T]); KROUT = di("KROUT", [2048, T])
        VRIN = di("VRIN", [NHB * T, 33]); VROUT = di("VROUT", [2 * NHB * T, 33])
        WQRD = di("WQRD", [D, 1024]); WKRD = di("WKRD", [D, 1024])
        WVRD = di("WVRD", [D, 512])
        WOFD = di("WOFD", [1024, D])

    sbp = lambda n, s, d: mk.ctx.enter_context(nc.sbuf_tensor(n, s, d))
    x = [sbp(f"x{i}", [128, T], BF16) for i in range(NIC)]
    xln = [sbp(f"xln{i}", [128, T], BF16) for i in range(NIC)]
    ones128 = sbp("ones128", [128, 1], BF16)
    oheads = sbp("oheads", [128, 2], BF16)
    onesPf = sbp("onesPf", [1, 128], F32)
    ones64f = sbp("ones64f", [64, 1], F32)
    i128 = sbp("i128", [128, 128], BF16)
    o128 = sbp("o128", [128, 128], BF16)

    sx = mk.sem("sx")
    sc0 = mk.sem("sc0")
    ev0 = mk.sem("ev0")
    mmv = mk.sem("mmv")
    ac0 = mk.sem("ac0")
    st0 = mk.sem("st0")

    with ExitStack() as pctx:
        sb = lambda n, s, d: pctx.enter_context(nc.sbuf_tensor(n, s, d))
        texp = sb("texp", [1, 1024], F32)
        texc = sb("texc", [1, 1023], BF16)
        fill_hi = sb("fill_hi", [1, 3584], BF16)
        fill_lo = sb("fill_lo", [1, 2048], BF16)
        onesrow = sb("onesrow", [1, 3584], BF16)
        maskr = sb("maskr", [1, 1536], U8)
        arow = sb("arow", [1, AW], BF16)
        relsb = sb("relsb", [64, 1023], F32)
        halfsb = sb("halfsb", [1, 1], F32)
        ps_t8 = pctx.enter_context(nc.psum_tensor("ps_t8", [1, 512], F32))
        ps_t8b = pctx.enter_context(nc.psum_tensor("ps_t8b", [1, 512], F32))

        with nc.Block("setup") as blk:
            @blk.gpsimd
            def _(g):
                for i in range(NIC):
                    sx.inc(g.dma_start(
                        x[i][:], xin.rearrange("(c p) t -> c p t", p=128)[i]), 16)
                sx.inc(g.dma_start(i128[:], iden[:]), 16)
                sx.wait(g)

            @blk.sync
            def _(sy):
                sc0.inc(sy.dma_start(relsb[:], relT[:]), 16)
                sc0.inc(sy.dma_start(halfsb[:], halfin[:]), 16)

            @blk.vector
            def _(v):
                v.memset(ones128[:], 1.0)
                v.memset(oheads[0:64, 0:1], 1.0)
                v.memset(oheads[64:128, 0:1], 0.0)
                v.memset(oheads[0:64, 1:2], 0.0)
                v.memset(oheads[64:128, 1:2], 1.0)
                v.memset(onesPf[:], 1.0)
                v.memset(o128[:], 1.0)
                v.memset(ones64f[:], 1.0 / 64.0)
                ev0.inc(v.memset(onesrow[:], 1.0))

            @blk.tensor
            def _(t):
                ev0.wait(t)
                sc0.wait(t)
                t.matmul(ps_t8[:], ones64f[:], relsb[:, 0:512], start=True, stop=True)
                mmv.inc(t.matmul(ps_t8b[0:1, 0:511], ones64f[:], relsb[:, 512:1023],
                                 start=True, stop=True))

            @blk.scalar
            def _(s):
                mmv.wait(s)
                s.activation(texp[0:1, 0:512], ps_t8[:], AF.Exp)
                ac0.inc(s.activation(texp[0:1, 512:1023], ps_t8b[0:1, 0:511], AF.Exp))

            @blk.vector
            def _(v):
                ac0.wait(v)
                v.tensor_copy(texc[:], texp[0:1, 0:1023])
                hi_ap = bass.AP(texp, 1022, [[1024, 1], [1, 1]])
                lo_ap = bass.AP(texp, 0, [[1024, 1], [1, 1]])
                v.tensor_scalar(fill_hi[:], onesrow[:], hi_ap, None, ALU.mult)
                v.tensor_scalar(fill_lo[:], onesrow[0:1, 0:2048], lo_ap, None, ALU.mult)
                sc0.wait(v)
                ev0.inc(v.tensor_scalar(maskr[:], onesrow[0:1, 0:1536], halfsb[:], None, ALU.mult))
                ev0.wait(v)
                v.tensor_copy(arow[0:1, 0:1537], fill_lo[0:1, 0:1537])
                v.tensor_copy(arow[0:1, 1537:2560], texc[:])
                v.tensor_copy(arow[0:1, 2560:4608], fill_hi[0:1, 0:2048])
                ev0.inc(v.tensor_copy(arow[0:1, 4608:AW], fill_hi[0:1, 0:AW - 4608]))
                ev0.wait(v)
                v.copy_predicated(arow[0:1, 1537:2560], maskr[0:1, 0:1023], fill_lo[0:1, 0:1023])
                v.copy_predicated(arow[0:1, 2560:3585], maskr[0:1, 0:1025], fill_lo[0:1, 0:1025])
                ev0.inc(v.copy_predicated(arow[0:1, 3585:4608], maskr[0:1, 0:1023], texc[:]))

            @blk.sync
            def _(sy):
                ev0.wait(sy)
                st0.inc(sy.dma_start(AD[:], arow[:]), 16)
                st0.wait(sy)

    if not PHASES.index("ln1") <= upto:
        _finish(nc, mk, x, tout)
        mk.ctx.close()
        return nc

    def emit_ln(tag, src, dst):
        sql = mk.sem("sql")
        mml = mk.sem("mml")
        evl = mk.sem("evl")
        acl = mk.sem("acl")
        b_sql, b_mml = sql.v, mml.v
        with ExitStack() as pctx:
            sb = lambda n, s, d: pctx.enter_context(nc.sbuf_tensor(n, s, d))
            xsq = [sb(f"xsq{tag}{i}", [128, T], BF16) for i in range(2)]
            mrow = sb(f"mrow{tag}", [1, T], F32)
            m2row = sb(f"m2row{tag}", [1, T], F32)
            vrow = sb(f"vrow{tag}", [1, T], F32)
            lnr = sb(f"lnr{tag}", [1, T], F32)
            arw = sb(f"arw{tag}", [1, T], F32)
            brw = sb(f"brw{tag}", [1, T], F32)
            ps = [pctx.enter_context(nc.psum_tensor(f"ps{tag}{i}", [128, 512], F32))
                  for i in range(8)]

            with nc.Block(f"ln{tag}") as blk:
                @blk.scalar
                def _(s):
                    for i in range(NIC):
                        if i >= 2:
                            s.wait_ge(mml.h, b_mml + i - 1)
                        sql.inc(s.activation(xsq[i % 2][:], src[i][:], AF.Square))

                @blk.tensor
                def _(t):
                    for i in range(NIC):
                        t.wait_ge(sql.h, b_sql + i + 1)
                        for tc in range(NTC):
                            t.matmul(ps[tc][0:1, :], ones128[:], src[i][:, tc * 512:(tc + 1) * 512],
                                     start=(i == 0), stop=(i == NIC - 1))
                        inst = None
                        for tc in range(NTC):
                            inst = t.matmul(ps[4 + tc][0:1, :], ones128[:],
                                            xsq[i % 2][:, tc * 512:(tc + 1) * 512],
                                            start=(i == 0), stop=(i == NIC - 1))
                        mml.inc(inst)

                @blk.vector
                def _(v):
                    mml.wait(v)
                    inst = None
                    for tc in range(NTC):
                        sl = slice(tc * 512, (tc + 1) * 512)
                        v.tensor_scalar(mrow[0:1, sl], ps[tc][0:1, :], 1.0 / D, None, ALU.mult)
                        inst = v.tensor_scalar(vrow[0:1, sl], ps[4 + tc][0:1, :], 1.0 / D, EPS,
                                               ALU.mult, ALU.add)
                    evl.inc(inst)

                @blk.scalar
                def _(s):
                    evl.wait(s)
                    acl.inc(s.activation(m2row[:], mrow[:], AF.Square))

                @blk.vector
                def _(v):
                    acl.wait(v)
                    evl.inc(v.tensor_sub(vrow[:], vrow[:], m2row[:]))

                @blk.scalar
                def _(s):
                    evl.wait(s)
                    acl.inc(s.activation(lnr[:], vrow[:], AF.Ln))
                    acl.wait(s)
                    acl.inc(s.activation(arw[:], lnr[:], AF.Exp, scale=-0.5))

                @blk.vector
                def _(v):
                    acl.wait(v)
                    evl.inc(v.scalar_tensor_tensor(brw[:], mrow[:], -1.0, arw[:],
                                                   ALU.mult, ALU.mult))

                @blk.tensor
                def _(t):
                    evl.wait(t)
                    inst = None
                    for tc in range(NTC):
                        sl = slice(tc * 512, (tc + 1) * 512)
                        t.matmul(ps[tc][:], onesPf[:], arw[0:1, sl], start=True, stop=True)
                        inst = t.matmul(ps[4 + tc][:], onesPf[:], brw[0:1, sl], start=True, stop=True)
                    mml.inc(inst)

                @blk.vector
                def _(v):
                    mml.wait(v)
                    inst = None
                    for i in range(NIC):
                        for tc in range(NTC):
                            sl = slice(tc * 512, (tc + 1) * 512)
                            inst = v.tensor_tensor(dst[i][:, sl], src[i][:, sl], ps[tc][:], ALU.mult)
                    evl.inc(inst)
                    evl.wait(v)
                    for i in range(NIC):
                        for tc in range(NTC):
                            sl = slice(tc * 512, (tc + 1) * 512)
                            v.tensor_tensor(dst[i][:, sl], dst[i][:, sl], ps[4 + tc][:], ALU.add)

    emit_ln("1", x, xln)
    if upto == PHASES.index("ln1"):
        _finish(nc, mk, xln, tout)
        mk.ctx.close()
        return nc

    def emit_aproj():
        swq = [mk.sem("swq0"), mk.sem("swq1")]
        mma = mk.sem("mma")
        eva = mk.sem("eva")
        stq = [mk.sem("stq0"), mk.sem("stq1")]
        stk = [mk.sem("stk0"), mk.sem("stk1")]
        stv = [mk.sem("stv0"), mk.sem("stv1")]
        ccv = mk.sem("ccv")
        b_mma, b_eva = mma.v, eva.v
        b_sw = [swq[0].v, swq[1].v]
        b_stq = [stq[0].v, stq[1].v]
        b_stk = [stk[0].v, stk[1].v]
        b_stv = [stv[0].v, stv[1].v]
        srcs = [wqT, wkT, wvT]
        NQ = 12

        def gid(qidx, j):
            return qidx * 16 + j

        with ExitStack() as pctx:
            sb = lambda n, s, d: pctx.enter_context(nc.sbuf_tensor(n, s, d))
            wbuf = [sb(f"wpa{i}", [128, 16 * 512], BF16) for i in range(2)]
            qst = [sb(f"qsta{i}", [128, T], BF16) for i in range(2)]
            vst = [sb(f"vsta{i}", [128, 512], BF16) for i in range(2)]
            pp = [pctx.enter_context(nc.psum_tensor(f"ppa{i}", [128, 512], F32))
                  for i in range(4)]

            with nc.Block("aproj") as blk:
                @blk.gpsimd
                def _(g):
                    for qidx in range(NQ):
                        w, oq = qidx // 4, qidx % 4
                        if qidx >= 2:
                            g.wait_ge(mma.h, b_mma + gid(qidx - 2, 16))
                        swq[qidx % 2].inc(g.dma_start(
                            wbuf[qidx % 2][:],
                            srcs[w][:, oq * 512:(oq + 1) * 512]
                            .rearrange("(ic p) o -> p ic o", p=128)), 16)

                @blk.tensor
                def _(t):
                    for qidx in range(NQ):
                        w, oq = qidx // 4, qidx % 4
                        t.wait_ge(swq[qidx % 2].h, b_sw[qidx % 2] + 16 * (qidx // 2 + 1))
                        for j in range(16):
                            g_ = gid(qidx, j)
                            bank = g_ % 4
                            if g_ >= 4:
                                t.wait_ge(eva.h, b_eva + g_ - 3)
                            inst = None
                            if w < 2:
                                oloc, tc = j // 4, j % 4
                                for ic in range(NIC):
                                    co = ic * 512 + oloc * 128
                                    inst = t.matmul(
                                        pp[bank][:], wbuf[qidx % 2][:, co:co + 128],
                                        xln[ic][:, tc * 512:(tc + 1) * 512],
                                        start=(ic == 0), stop=(ic == NIC - 1))
                            else:
                                tt = j
                                for ic in range(NIC):
                                    inst = t.matmul(
                                        pp[bank][:], xln[ic][:, tt * 128:(tt + 1) * 128],
                                        wbuf[qidx % 2][:, ic * 512:(ic + 1) * 512],
                                        start=(ic == 0), stop=(ic == NIC - 1))
                            mma.inc(inst)

                @blk.scalar
                def _(s):
                    for qidx in range(NQ):
                        w, oq = qidx // 4, qidx % 4
                        for j in range(16):
                            g_ = gid(qidx, j)
                            s.wait_ge(mma.h, b_mma + g_ + 1)
                            if w < 2:
                                oloc, tc = j // 4, j % 4
                                oc = oq * 4 + oloc
                                if tc == 0:
                                    if w == 0 and oc >= 2:
                                        s.wait_ge(stq[oc % 2].h, b_stq[oc % 2] + 16 * (oc // 2))
                                    elif w == 1:
                                        if oc < 2:
                                            s.wait_ge(stq[oc % 2].h, b_stq[oc % 2] + 16 * 8)
                                        else:
                                            s.wait_ge(stk[oc % 2].h, b_stk[oc % 2] + 16 * (oc // 2))
                                eva.inc(s.copy(qst[oc % 2][:, tc * 512:(tc + 1) * 512],
                                               pp[g_ % 4][:]))
                            else:
                                sv = (qidx - 8) * 16 + j
                                if sv >= 2:
                                    s.wait_ge(stv[sv % 2].h, b_stv[sv % 2] + 16 * (sv // 2))
                                eva.inc(s.copy(vst[sv % 2][:], pp[g_ % 4][:]))

                @blk.sync
                def _(sy):
                    for qidx in range(NQ):
                        w, oq = qidx // 4, qidx % 4
                        for j in range(16):
                            g_ = gid(qidx, j)
                            if w < 2:
                                oloc, tc = j // 4, j % 4
                                if tc != 3:
                                    continue
                                oc = oq * 4 + oloc
                                sy.wait_ge(eva.h, b_eva + g_ + 1)
                                if w == 0:
                                    stq[oc % 2].inc(sy.dma_start(
                                        QD.rearrange("(c p) t -> c p t", p=128)[oc],
                                        qst[oc % 2][:]), 16)
                                else:
                                    stk[oc % 2].inc(sy.dma_start(
                                        KVIN.rearrange("(c p) t -> c p t", p=128)[oc],
                                        qst[oc % 2][:]), 16)
                            else:
                                tt = j
                                sv = (qidx - 8) * 16 + j
                                sy.wait_ge(eva.h, b_eva + g_ + 1)
                                stv[sv % 2].inc(sy.dma_start(
                                    KVIN[D + tt * 128:D + (tt + 1) * 128,
                                         oq * 512:(oq + 1) * 512],
                                    vst[sv % 2][:]), 16)

                @blk.gpsimd
                def _(g):
                    stk[0].wait(g); stk[1].wait(g)
                    stv[0].wait(g); stv[1].wait(g)
                    ccv.inc(g.collective_compute(
                        "AllGather", ALU.bypass,
                        replica_groups=[[0, 1], [2, 3], [4, 5], [6, 7]],
                        ins=[KVIN.ap().opt()], outs=[KVOUT.ap().opt()]))

    emit_aproj()
    if upto == PHASES.index("aproj"):
        _finish(nc, mk, x, tout)
        mk.ctx.close()
        return nc

    def emit_acore():
        ldh = [mk.sem("ldh0"), mk.sem("ldh1")]
        stm = mk.sem("stm")
        sqs = mk.sem("sqs")
        bse = mk.sem("bse")
        cte = mk.sem("cte")
        ms3 = mk.sem("ms3")
        dvs = mk.sem("dvs")
        mmq = mk.sem("mmq")
        mmav = mk.sem("mmav")
        ac3 = mk.sem("ac3")
        dv3 = mk.sem("dv3")
        rep = mk.sem("rep")
        bcm = mk.sem("bcm")
        epc = mk.sem("epc")
        ctm = mk.sem("ctm")
        von = mk.sem("von")
        ccv = mk.sems["ccv"]
        RING = 6
        b3 = {n: mk.sems[n].v for n in
              ("sqs", "bse", "cte", "ms3", "dvs", "mmq", "mmav", "ac3", "dv3",
               "rep", "bcm", "epc", "ctm")}
        b_ldh = [ldh[0].v, ldh[1].v]

        with ExitStack() as pctx:
            sb = lambda n, s, d: pctx.enter_context(nc.sbuf_tensor(n, s, d))
            M = sb("Mst", [128, MW + 1], BF16)
            ktile = [sb(f"kt3{i}", [128, S], BF16) for i in range(2)]
            qtile = [sb(f"qt3{i}", [128, T], BF16) for i in range(2)]
            vbuf = [[sb(f"vb3{i}{h}", [128, 32 * 65], BF16) for h in range(2)]
                    for i in range(2)]
            sqt = sb("sqt3", [128, T], BF16)
            probs = sb("probs3", [128, 3 * 1024], BF16)
            ctxr = sb("ctxr3", [64, 512], BF16)
            ctx_st = [sb(f"cst3{i}", [128, 512], BF16) for i in range(4)]
            mstat = sb("mstat3", [1, 40], F32)
            rrow = sb("rrow3", [1, 512], F32)
            bias_h = [sb(f"bh3{i}", [128, 1], F32) for i in range(2)]
            qk_ps = [pctx.enter_context(nc.psum_tensor(f"qk3{i}", [128, 1024], F32))
                     for i in range(2)]
            av_ps = pctx.enter_context(nc.psum_tensor("av3", [65, 512], F32))
            bc_ps = pctx.enter_context(nc.psum_tensor("bc3", [64, 512], F32))
            st_ps = [qk_ps[0][0:1, 0:512], qk_ps[0][0:1, 512:1024],
                     qk_ps[1][0:1, 0:512], qk_ps[1][0:1, 512:1024]]
            stb_ps = [qk_ps[0][0:128, 0:1], qk_ps[1][0:128, 0:1]]

            ROUNDS = [(0, 0, 0), (1, 0, 12), (0, 1, 4), (1, 1, 16), (0, 2, 8), (1, 2, 20)]

            def widx(hp, hloc, qc, kt=0):
                return hp * 256 + (hloc * 4 + qc) * 32 + kt

            def grp(hp, hloc, qc):
                return hp * 8 + hloc * 4 + qc

            with nc.Block("acore") as blk:
                @blk.sync
                def _(sy):
                    stm.inc(sy.dma_start(M[:], bass.AP(AD, 0, [[1, 128], [1, MW + 1]])), 16)

                    def stores(hp):
                        sy.wait_ge(epc.h, b3["epc"] + hp * 8 + 8)
                        for qc in range(4):
                            ctm.inc(sy.dma_start(
                                CTXD[hp * 128:(hp + 1) * 128, qc * 512:(qc + 1) * 512],
                                ctx_st[qc][:]), 16)

                    for hp in range(NTT):
                        par = hp % 2
                        if hp >= 2:
                            sy.wait_ge(mmav.h, b3["mmav"] + (hp - 1) * 256)
                            sy.wait_ge(sqs.h, b3["sqs"] + (hp - 1) * 5)
                        if hp == 0:
                            ccv.wait(sy)
                        ldh[par].inc(sy.dma_start(
                            qtile[par][:], QD[hp * 128:(hp + 1) * 128, :]), 16)
                        for r in range(2):
                            ldh[par].inc(sy.dma_start(
                                ktile[par][:, r * T:(r + 1) * T],
                                KVOUT[r * 2 * D + hp * 128: r * 2 * D + (hp + 1) * 128, :]), 16)
                        for hloc in range(2):
                            for r in range(2):
                                src = KVOUT[r * 2 * D + D: r * 2 * D + 2 * D,
                                            (2 * hp + hloc) * 64:(2 * hp + hloc) * 64 + 64]
                                ldh[par].inc(sy.dma_start(
                                    vbuf[par][hloc].rearrange("p (kt w) -> p kt w", w=65)
                                    [:, r * 16:(r + 1) * 16, 0:64],
                                    src.rearrange("(tt p) d -> p tt d", p=128)), 16)
                        if hp >= 1:
                            stores(hp - 1)
                    stores(NTT - 1)
                    ctm.wait(sy)

                @blk.vector
                def _(v):
                    inst = None
                    for par in range(2):
                        for h in range(2):
                            inst = v.memset(
                                vbuf[par][h].rearrange("p (kt w) -> p kt w", w=65)[:, :, 64:65],
                                1.0)
                    von.inc(inst)
                    for hp in range(NTT):
                        for r, (hloc, sqi, colb) in enumerate(ROUNDS):
                            v.wait_ge(ms3.h, b3["ms3"] + hp * 7 + r + 1)
                            inst = None
                            for tc in range(4):
                                inst = v.reduce_max(mstat[0:1, colb + tc:colb + tc + 1],
                                                    st_ps[tc], AX.X)
                            dvs.inc(inst)
                        dvs.wait(v)
                        inst = None
                        for hloc in range(2):
                            qb = hloc * 12
                            v.reduce_max(mstat[0:1, 24 + 2 * hloc:25 + 2 * hloc],
                                         mstat[0:1, qb:qb + 4], AX.X)
                            inst = v.reduce_max(mstat[0:1, 25 + 2 * hloc:26 + 2 * hloc],
                                                mstat[0:1, qb + 4:qb + 12], AX.X)
                        dvs.inc(inst)
                        v.wait_ge(sqs.h, b3["sqs"] + hp * 5 + 4)
                        v.tensor_tensor(mstat[0:1, 32:33], mstat[0:1, 28:29],
                                        mstat[0:1, 29:30], ALU.add)
                        dvs.inc(v.tensor_tensor(mstat[0:1, 33:34], mstat[0:1, 30:31],
                                                mstat[0:1, 31:32], ALU.add))
                        v.wait_ge(sqs.h, b3["sqs"] + hp * 5 + 5)
                        dvs.inc(v.tensor_scalar(mstat[0:1, 36:38], mstat[0:1, 34:36],
                                                -1.0 / 8.0, None, ALU.mult))
                        for hloc in range(2):
                            for qc in range(4):
                                g_ = grp(hp, hloc, qc)
                                for pr in range(16):
                                    idx0 = widx(hp, hloc, qc, 2 * pr)
                                    pidx = idx0 // 2
                                    v.wait_ge(ac3.h, b3["ac3"] + pidx + 1)
                                    slot = pidx % 3
                                    j0 = MOFF - (2 * pr) * 128 + qc * 512
                                    e_ap = bass.AP(M, MW - j0, [[MW + 1, 128], [128, 2], [-1, 512]])
                                    pslot = probs[:, slot * 1024:(slot + 1) * 1024]                                        .rearrange("p (j q) -> p j q", q=512)
                                    dv3.inc(v.tensor_tensor(pslot, pslot, e_ap, ALU.mult))
                                v.wait_ge(mmav.h, b3["mmav"] + (g_ + 1) * 32)
                                if g_ >= 1:
                                    v.wait_ge(bcm.h, b3["bcm"] + g_)
                                rep.inc(v.reciprocal(rrow[:], av_ps[64:65, :]))
                                v.wait_ge(bcm.h, b3["bcm"] + g_ + 1)
                                v.wait_ge(cte.h, b3["cte"] + g_ + 1)
                                if hloc == 0 and qc == 0:
                                    v.wait_ge(ctm.h, b3["ctm"] + hp * 64)
                                epc.inc(v.tensor_tensor(
                                    ctx_st[qc][64 * hloc:64 * hloc + 64, :],
                                    ctxr[:], bc_ps[:], ALU.mult))

                @blk.scalar
                def _(s):
                    for hp in range(NTT):
                        par = hp % 2
                        s.wait_ge(ldh[par].h, b_ldh[par] + 112 * (hp // 2 + 1))
                        if hp >= 1:
                            s.wait_ge(ms3.h, b3["ms3"] + hp * 7 - 1)
                        sqs.inc(s.activation(sqt[:], qtile[par][:], AF.Square))
                        s.wait_ge(ms3.h, b3["ms3"] + hp * 7 + 2)
                        sqs.inc(s.activation(sqt[:], ktile[par][:, 0:T], AF.Square))
                        s.wait_ge(ms3.h, b3["ms3"] + hp * 7 + 4)
                        sqs.inc(s.activation(sqt[:], ktile[par][:, T:S], AF.Square))
                        s.wait_ge(dvs.h, b3["dvs"] + hp * 9 + 7)
                        sqs.inc(s.activation(mstat[0:1, 28:32], mstat[0:1, 24:28], AF.Ln))
                        s.wait_ge(dvs.h, b3["dvs"] + hp * 9 + 8)
                        sqs.inc(s.activation(mstat[0:1, 34:36], mstat[0:1, 32:34],
                                             AF.Exp, scale=0.5))
                        s.wait_ge(ms3.h, b3["ms3"] + hp * 7 + 7)
                        if hp >= 1:
                            s.wait_ge(ac3.h, b3["ac3"] + hp * 128)
                        bse.inc(s.copy(bias_h[0][:], stb_ps[0]))
                        bse.inc(s.copy(bias_h[1][:], stb_ps[1]))
                        for hloc in range(2):
                            for qc in range(4):
                                g_ = grp(hp, hloc, qc)
                                for pr in range(16):
                                    idx0 = widx(hp, hloc, qc, 2 * pr)
                                    pidx = idx0 // 2
                                    s.wait_ge(mmq.h, b3["mmq"] + idx0 + 2)
                                    if pidx >= 3:
                                        s.wait_ge(mmav.h, b3["mmav"] + (pidx - 3) * 2 + 2)
                                    slot = pidx % 3
                                    ac3.inc(s.activation(
                                        probs[:, slot * 1024:(slot + 1) * 1024],
                                        qk_ps[pidx % 2][:], AF.Exp, scale=0.125,
                                        bias=bias_h[hloc][:]))
                                s.wait_ge(mmav.h, b3["mmav"] + (g_ + 1) * 32)
                                if g_ >= 1:
                                    s.wait_ge(epc.h, b3["epc"] + g_)
                                cte.inc(s.copy(ctxr[:], av_ps[0:64, :]))

                @blk.tensor
                def _(t):
                    stm.wait(t)
                    von.wait(t)
                    for hp in range(NTT):
                        par = hp % 2
                        for r, (hloc, sqi, colb) in enumerate(ROUNDS):
                            t.wait_ge(sqs.h, b3["sqs"] + hp * 5 + sqi + 1)
                            if r >= 1:
                                t.wait_ge(dvs.h, b3["dvs"] + hp * 9 + r)
                            inst = None
                            for tc in range(4):
                                inst = t.matmul(st_ps[tc],
                                                oheads[:, hloc:hloc + 1],
                                                sqt[:, tc * 512:(tc + 1) * 512],
                                                start=True, stop=True)
                            ms3.inc(inst)
                        t.wait_ge(dvs.h, b3["dvs"] + hp * 9 + 9)
                        if hp >= 1:
                            t.wait_ge(bse.h, b3["bse"] + hp * 2)
                        t.matmul(stb_ps[0], onesPf[:], mstat[0:1, 36:37],
                                 start=True, stop=True)
                        ms3.inc(t.matmul(stb_ps[1], onesPf[:], mstat[0:1, 37:38],
                                         start=True, stop=True))
                        for hloc in range(2):
                            hb = 64 * hloc
                            for qc in range(4):
                                g_ = grp(hp, hloc, qc)

                                def av(kt):
                                    idx = widx(hp, hloc, qc, kt)
                                    pidx = idx // 2
                                    t.wait_ge(dv3.h, b3["dv3"] + pidx + 1)
                                    if kt == 0 and g_ >= 1:
                                        t.wait_ge(rep.h, b3["rep"] + g_)
                                        t.wait_ge(cte.h, b3["cte"] + g_)
                                    mmav.inc(t.matmul(
                                        av_ps[:],
                                        vbuf[par][hloc][:, kt * 65:kt * 65 + 65],
                                        probs[:, (pidx % 3) * 1024 + (kt % 2) * 512:
                                              (pidx % 3) * 1024 + (kt % 2) * 512 + 512],
                                        start=(kt == 0), stop=(kt == 31)))

                                for kt in range(32):
                                    idx = widx(hp, hloc, qc, kt)
                                    pidx = idx // 2
                                    if kt % 2 == 0 and pidx >= 2:
                                        t.wait_ge(ac3.h, b3["ac3"] + pidx - 1)
                                    if hloc == 0 and qc == 0 and kt == 0:
                                        t.wait_ge(bse.h, b3["bse"] + hp * 2 + 2)
                                    mmq.inc(t.matmul(
                                        qk_ps[pidx % 2][:, (kt % 2) * 512:(kt % 2) * 512 + 512],
                                        ktile[par][hb:hb + 64, kt * 128:(kt + 1) * 128],
                                        qtile[par][hb:hb + 64, qc * 512:(qc + 1) * 512],
                                        start=True, stop=True))
                                    if kt >= 3:
                                        av(kt - 3)
                                av(29)
                                av(30)
                                av(31)
                                t.wait_ge(rep.h, b3["rep"] + g_ + 1)
                                if g_ >= 1:
                                    t.wait_ge(epc.h, b3["epc"] + g_)
                                bcm.inc(t.matmul(bc_ps[:], onesPf[0:1, 0:64], rrow[:],
                                                 start=True, stop=True))

    emit_acore()
    if upto == PHASES.index("acore"):
        _finish(nc, mk, x, tout)
        mk.ctx.close()
        return nc

    def emit_aout():
        swo = [mk.sem("swo0"), mk.sem("swo1")]
        ldc = [mk.sem("ldc0"), mk.sem("ldc1")]
        mmo = mk.sem("mmo")
        dvo = mk.sem("dvo")
        b_mmo, b_dvo = mmo.v, dvo.v
        b_swo = [swo[0].v, swo[1].v]
        b_ldc = [ldc[0].v, ldc[1].v]
        with ExitStack() as pctx:
            sb = lambda n, s, d: pctx.enter_context(nc.sbuf_tensor(n, s, d))
            wbuf = [sb(f"wpo{i}", [128, 16 * 512], BF16) for i in range(2)]
            cbuf = [sb(f"cbo{i}", [128, 16 * 512], BF16) for i in range(2)]
            po = [pctx.enter_context(nc.psum_tensor(f"po4{i}", [128, 512], F32))
                  for i in range(4)]
            with nc.Block("aout") as blk:
                @blk.gpsimd
                def _(g):
                    for oq in range(4):
                        if oq >= 2:
                            g.wait_ge(mmo.h, b_mmo + (oq - 1) * 16)
                        swo[oq % 2].inc(g.dma_start(
                            wbuf[oq % 2][:],
                            woT[:, oq * 512:(oq + 1) * 512]
                            .rearrange("(ic p) o -> p ic o", p=128)), 16)

                @blk.sync
                def _(sy):
                    for oq in range(4):
                        for tc in range(4):
                            li = oq * 4 + tc
                            if li >= 2:
                                sy.wait_ge(mmo.h, b_mmo + (li - 1) * 4)
                            ldc[li % 2].inc(sy.dma_start(
                                cbuf[li % 2][:],
                                CTXD[:, tc * 512:(tc + 1) * 512]
                                .rearrange("(fc p) t -> p fc t", p=128)), 16)

                @blk.tensor
                def _(t):
                    for oq in range(4):
                        t.wait_ge(swo[oq % 2].h, b_swo[oq % 2] + 16 * (oq // 2 + 1))
                        for tc in range(4):
                            li = oq * 4 + tc
                            t.wait_ge(ldc[li % 2].h, b_ldc[li % 2] + 16 * (li // 2 + 1))
                            for oloc in range(4):
                                oc = oq * 4 + oloc
                                gi = li * 4 + oloc
                                bank = gi % 4
                                if gi >= 4:
                                    t.wait_ge(dvo.h, b_dvo + gi - 3)
                                inst = None
                                for fc in range(NIC):
                                    co = fc * 512 + oloc * 128
                                    inst = t.matmul(
                                        po[bank][:], wbuf[oq % 2][:, co:co + 128],
                                        cbuf[li % 2][:, fc * 512:(fc + 1) * 512],
                                        start=(fc == 0), stop=(fc == NIC - 1))
                                mmo.inc(inst)

                @blk.vector
                def _(v):
                    for oq in range(4):
                        for tc in range(4):
                            for oloc in range(4):
                                oc = oq * 4 + oloc
                                gi = (oq * 4 + tc) * 4 + oloc
                                v.wait_ge(mmo.h, b_mmo + gi + 1)
                                sl = slice(tc * 512, (tc + 1) * 512)
                                dvo.inc(v.tensor_tensor(x[oc][:, sl], x[oc][:, sl],
                                                        po[gi % 4][:], ALU.add))

    emit_aout()
    if upto == PHASES.index("aout"):
        _finish(nc, mk, x, tout)
        mk.ctx.close()
        return nc

    emit_ln("2", x, xln)
    if upto == PHASES.index("ln2"):
        _finish(nc, mk, xln, tout)
        mk.ctx.close()
        return nc

    def emit_ffn(tag):
        wsf = [mk.sem("wsf0"), mk.sem("wsf1")]
        w2s = [mk.sem("w2s0"), mk.sem("w2s1")]
        wrs = mk.sem("wrs")
        mmr = mk.sem("mmr")
        gse = mk.sem("gse")
        gsd = mk.sem("gsd")
        acr = mk.sem("acr")
        mmu = mk.sem("mmu")
        mmg = mk.sem("mmg")
        mmo2 = mk.sem("mmo2")
        acS = mk.sem("acS")
        dvT = mk.sem("dvT")
        dvF = mk.sem("dvF")
        dvX = mk.sem("dvX")
        dvB = mk.sem("dvB")
        b = {n: mk.sems[n].v for n in
             ("mmr", "gse", "gsd", "acr", "mmu", "mmg", "mmo2", "acS",
              "dvT", "dvF", "dvX", "dvB", "wrs")}
        b_wsf = [wsf[0].v, wsf[1].v]
        b_w2s = [w2s[0].v, w2s[1].v]
        NH = len(HCH)
        ETCP = [(e, tcp) for e in range(-1, NE) for tcp in range(2)]

        with ExitStack() as pctx:
            sb = lambda n, s, d: pctx.enter_context(nc.sbuf_tensor(n, s, d))
            hbuf = [sb(f"hb{tag}_{i}", [128, 1024], BF16) for i in range(NH)]
            w1b = [sb(f"w1b{tag}{i}", [128, 16 * 128], BF16) for i in range(2)]
            w3b = [sb(f"w3b{tag}{i}", [128, 16 * 128], BF16) for i in range(2)]
            w2b = [sb(f"w2b{tag}{i}", [128, NH * 128], BF16) for i in range(2)]
            wrb = sb(f"wrb{tag}", [128, 16 * 8], BF16)
            sub = [sb(f"su{tag}{i}", [128, 512], BF16) for i in range(2)]
            tmb = [sb(f"tm{tag}{i}", [128, 512], BF16) for i in range(2)]
            bcolb = [sb(f"bcl{tag}{i}", [128, 512], BF16) for i in range(2)]
            gates = sb(f"gt{tag}", [128, 128], F32)
            p8b = sb(f"p8{tag}", [128, 8], F32)
            oh1b = sb(f"oh1{tag}", [128, 8], F32)
            oh2b = sb(f"oh2{tag}", [128, 8], F32)
            p2b = sb(f"p2{tag}", [128, 8], F32)
            tpb = sb(f"tp{tag}", [128, 8], F32)
            m1b = sb(f"m1{tag}", [128, 1], F32)
            m2b = sb(f"m2{tag}", [128, 1], F32)
            nmb = sb(f"nm{tag}", [128, 1], F32)
            deb = sb(f"de{tag}", [128, 1], F32)
            reb = sb(f"re{tag}", [128, 1], F32)
            pu = [pctx.enter_context(nc.psum_tensor(f"pu{tag}{i}", [128, 512], F32)) for i in range(2)]
            pv = [pctx.enter_context(nc.psum_tensor(f"pv{tag}{i}", [128, 512], F32)) for i in range(2)]
            pg = [pctx.enter_context(nc.psum_tensor(f"pg{tag}{i}", [128, 512], F32)) for i in range(2)]
            po = [pctx.enter_context(nc.psum_tensor(f"po{tag}{i}", [128, 512], F32)) for i in range(2)]

            def w13src(e, hb):
                if e < 0:
                    return (sw1R[hb], sw3R[hb])
                return (ew1R[e, hb], ew3R[e, hb])

            def w2src(e, oc):
                return sw2R[oc] if e < 0 else ew2R[e, oc]

            with nc.Block(f"ffn{tag}") as blk:
                @blk.gpsimd
                def _(g):
                    wrs.inc(g.dma_start(
                        wrb.rearrange("p (ic w) -> p ic w", w=8),
                        wrTp[:, 0:8].rearrange("(ic p) w -> p ic w", p=128)), 16)
                    for ei_, (e, tcp) in enumerate(ETCP):
                        for hb, (hb0, hn) in enumerate(HCH):
                            lb = ei_ * NH + hb
                            par = lb % 2
                            if lb >= 2:
                                g.wait_ge(mmu.h, b["mmu"] + (lb - 1) * 4)
                            s1, s3 = w13src(e, hb)
                            wsf[par].inc(g.dma_start(w1b[par][:], s1), 16)
                            wsf[par].inc(g.dma_start(w3b[par][:], s3), 16)
                        for oc in range(16):
                            ob = ei_ * 16 + oc
                            par = ob % 2
                            if ob >= 2:
                                g.wait_ge(mmo2.h, b["mmo2"] + (ob - 1) * 2)
                            w2s[par].inc(g.dma_start(w2b[par][:], w2src(e, oc)), 16)

                @blk.tensor
                def _(t):
                    wrs.wait(t)
                    for tt in range(16):
                        if tt >= 2:
                            t.wait_ge(gse.h, b["gse"] + tt - 1)
                        inst = None
                        for ic in range(NIC):
                            inst = t.matmul(po[tt % 2][:, 0:8],
                                            xln[ic][:, tt * 128:(tt + 1) * 128],
                                            wrb[:, ic * 8:ic * 8 + 8],
                                            start=(ic == 0), stop=(ic == NIC - 1))
                        mmr.inc(inst)
                    for ei_, (e, tcp) in enumerate(ETCP):
                        if e >= 0:
                            if ei_ >= 1:
                                t.wait_ge(dvF.h, b["dvF"] + ei_ * 44)
                            for t5loc in range(2):
                                gix = (ei_ - 2) * 2 + t5loc
                                inst = None
                                for ttl in range(4):
                                    bix = (ei_ - 2) * 8 + t5loc * 4 + ttl
                                    t.wait_ge(dvB.h, b["dvB"] + bix + 1)
                                    inst = t.matmul(pg[t5loc][:, ttl * 128:(ttl + 1) * 128],
                                                    bcolb[t5loc][:, ttl * 128:(ttl + 1) * 128],
                                                    i128[:], start=True, stop=True)
                                mmg.inc(inst)
                        for hb, (hb0, hn) in enumerate(HCH):
                            lb = ei_ * NH + hb
                            par = lb % 2
                            t.wait_ge(wsf[par].h, b_wsf[par] + 32 * (lb // 2 + 1))
                            for t5loc in range(2):
                                t5 = tcp * 2 + t5loc
                                gu = ei_ * NH * 4 + hb * 4 + t5loc * 2
                                sidx = ei_ * 44 + hb * 2 + t5loc
                                if sidx >= 2:
                                    t.wait_ge(acS.h, b["acS"] + sidx - 1)
                                    t.wait_ge(dvT.h, b["dvT"] + sidx - 1)
                                inst = None
                                for ic in range(NIC):
                                    inst = t.matmul(pu[t5loc][0:hn, :],
                                                    w1b[par][:, ic * 128:ic * 128 + hn],
                                                    xln[ic][:, t5 * 512:(t5 + 1) * 512],
                                                    start=(ic == 0), stop=(ic == NIC - 1))
                                mmu.inc(inst)
                                inst = None
                                for ic in range(NIC):
                                    inst = t.matmul(pv[t5loc][0:hn, :],
                                                    w3b[par][:, ic * 128:ic * 128 + hn],
                                                    xln[ic][:, t5 * 512:(t5 + 1) * 512],
                                                    start=(ic == 0), stop=(ic == NIC - 1))
                                mmu.inc(inst)
                        for oc in range(16):
                            ob = ei_ * 16 + oc
                            par = ob % 2
                            t.wait_ge(w2s[par].h, b_w2s[par] + 16 * (ob // 2 + 1))
                            if oc == 0:
                                t.wait_ge(dvF.h, b["dvF"] + (ei_ + 1) * 44)
                                if ei_ == 0:
                                    t.wait_ge(gse.h, b["gse"] + 16)
                            for t5loc in range(2):
                                og = ei_ * 32 + oc * 2 + t5loc
                                if og >= 2:
                                    t.wait_ge(dvX.h, b["dvX"] + og - 1)
                                inst = None
                                for hc, (hb0, hn) in enumerate(HCH):
                                    inst = t.matmul(po[og % 2][:],
                                                    w2b[par][0:hn, hc * 128:(hc + 1) * 128],
                                                    hbuf[hc][0:hn, t5loc * 512:(t5loc + 1) * 512],
                                                    start=(hc == 0), stop=(hc == NH - 1))
                                mmo2.inc(inst)

                @blk.scalar
                def _(s):
                    for tt in range(16):
                        s.wait_ge(mmr.h, b["mmr"] + tt + 1)
                        s.wait_ge(gsd.h, b["gsd"] + tt * 11 + 1)
                        acr.inc(s.activation(p8b[:], po[tt % 2][:, 0:8], AF.Exp,
                                             bias=nmb[:]))
                    for ei_, (e, tcp) in enumerate(ETCP):
                        for hb, (hb0, hn) in enumerate(HCH):
                            for t5loc in range(2):
                                gu = ei_ * NH * 4 + hb * 4 + t5loc * 2
                                sidx = ei_ * 44 + hb * 2 + t5loc
                                s.wait_ge(mmu.h, b["mmu"] + gu + 1)
                                if sidx >= 2:
                                    s.wait_ge(dvT.h, b["dvT"] + sidx - 1)
                                acS.inc(s.activation(sub[t5loc][0:hn, :],
                                                     pu[t5loc][0:hn, :], AF.Silu))

                @blk.vector
                def _(v):
                    for tt in range(16):
                        c0 = b["gsd"] + tt * 11
                        v.wait_ge(mmr.h, b["mmr"] + tt + 1)
                        if tt >= 1:
                            v.wait_ge(gse.h, b["gse"] + tt)
                            v.wait_ge(acr.h, b["acr"] + tt)
                        gsd.inc(v.reduce_max(nmb[:], po[tt % 2][:, 0:8], AX.X, negate=True))
                        v.wait_ge(acr.h, b["acr"] + tt + 1)
                        gsd.inc(v.reduce_max(m1b[:], p8b[:], AX.X))
                        v.wait_ge(gsd.h, c0 + 2)
                        gsd.inc(v.tensor_scalar(oh1b[:], p8b[:], m1b[:], None, ALU.is_ge))
                        v.wait_ge(gsd.h, c0 + 3)
                        gsd.inc(v.tensor_mul(tpb[:], p8b[:], oh1b[:]))
                        v.wait_ge(gsd.h, c0 + 4)
                        gsd.inc(v.tensor_sub(p2b[:], p8b[:], tpb[:]))
                        v.wait_ge(gsd.h, c0 + 5)
                        gsd.inc(v.reduce_max(m2b[:], p2b[:], AX.X))
                        v.wait_ge(gsd.h, c0 + 6)
                        gsd.inc(v.tensor_scalar(oh2b[:], p2b[:], m2b[:], None, ALU.is_ge))
                        v.wait_ge(gsd.h, c0 + 7)
                        gsd.inc(v.tensor_add(oh1b[:], oh1b[:], oh2b[:]))
                        gsd.inc(v.tensor_add(deb[:], m1b[:], m2b[:]))
                        v.wait_ge(gsd.h, c0 + 9)
                        gsd.inc(v.reciprocal(reb[:], deb[:]))
                        v.wait_ge(gsd.h, c0 + 8)
                        gsd.inc(v.tensor_mul(tpb[:], p8b[:], oh1b[:]))
                        v.wait_ge(gsd.h, c0 + 11)
                        gse.inc(v.tensor_scalar(gates[:, tt * 8:(tt + 1) * 8], tpb[:],
                                                reb[:], None, ALU.mult))
                    for ei_, (e, tcp) in enumerate(ETCP):
                        if e >= 0:
                            for t5loc in range(2):
                                t5 = tcp * 2 + t5loc
                                for ttl in range(4):
                                    tt = t5 * 4 + ttl
                                    bix = (ei_ - 2) * 8 + t5loc * 4 + ttl
                                    if bix == 0:
                                        v.wait_ge(gse.h, b["gse"] + 16)
                                    if ei_ >= 3:
                                        v.wait_ge(mmg.h, b["mmg"] + (ei_ - 2) * 2)
                                    dvB.inc(v.tensor_scalar(
                                        bcolb[t5loc][:, ttl * 128:(ttl + 1) * 128], o128[:],
                                        gates[:, tt * 8 + e: tt * 8 + e + 1], None, ALU.mult))
                        for hb, (hb0, hn) in enumerate(HCH):
                            for t5loc in range(2):
                                gu = ei_ * NH * 4 + hb * 4 + t5loc * 2
                                sidx = ei_ * 44 + hb * 2 + t5loc
                                v.wait_ge(acS.h, b["acS"] + sidx + 1)
                                v.wait_ge(mmu.h, b["mmu"] + gu + 2)
                                dvT.inc(v.tensor_mul(tmb[t5loc][0:hn, :],
                                                     sub[t5loc][0:hn, :],
                                                     pv[t5loc][0:hn, :]))
                                v.wait_ge(dvT.h, b["dvT"] + sidx + 1)
                                if ei_ >= 1 and hb == 0 and t5loc == 0:
                                    v.wait_ge(mmo2.h, b["mmo2"] + ei_ * 32)
                                if e >= 0:
                                    gix = (ei_ - 2) * 2 + t5loc
                                    v.wait_ge(mmg.h, b["mmg"] + gix + 1)
                                    dvF.inc(v.tensor_mul(
                                        hbuf[hb][0:hn, t5loc * 512:(t5loc + 1) * 512],
                                        tmb[t5loc][0:hn, :], pg[t5loc][0:hn, :]))
                                else:
                                    dvF.inc(v.tensor_copy(
                                        hbuf[hb][0:hn, t5loc * 512:(t5loc + 1) * 512],
                                        tmb[t5loc][0:hn, :]))
                        for oc in range(16):
                            for t5loc in range(2):
                                t5 = tcp * 2 + t5loc
                                og = ei_ * 32 + oc * 2 + t5loc
                                v.wait_ge(mmo2.h, b["mmo2"] + og + 1)
                                sl = slice(t5 * 512, (t5 + 1) * 512)
                                dvX.inc(v.tensor_tensor(x[oc][:, sl], x[oc][:, sl],
                                                        po[og % 2][:], ALU.add))

    emit_ffn("f1")
    if upto == PHASES.index("ffn1"):
        _finish(nc, mk, x, tout)
        mk.ctx.close()
        return nc

    emit_ln("3", x, xln)
    if upto == PHASES.index("ln3"):
        _finish(nc, mk, xln, tout)
        mk.ctx.close()
        return nc

    def emit_bprep():
        swb = [mk.sem("swb0"), mk.sem("swb1")]
        swd = mk.sem("swd")
        mmf = mk.sem("mmf")
        acf = mk.sem("acf")
        dvz = mk.sem("dvz")
        stw = mk.sem("stw")
        stwk = mk.sem("stwk")
        stwo = mk.sem("stwo")
        stwv = mk.sem("stwv")
        swr = [mk.sem("swr0"), mk.sem("swr1")]
        swv = mk.sem("swv")
        mmp = mk.sem("mmp")
        acp = mk.sem("acp")
        stp = [mk.sem("stp0"), mk.sem("stp1")]
        ccb = mk.sem("ccb")
        bb = {n: mk.sems[n].v for n in
              ("swd", "mmf", "acf", "dvz", "stw", "stwk", "stwo", "stwv", "swv", "mmp", "acp", "ccb")}
        b_swb = [swb[0].v, swb[1].v]
        b_swr = [swr[0].v, swr[1].v]
        b_stp = [stp[0].v, stp[1].v]

        with ExitStack() as pctx:
            sb = lambda n, s, d: pctx.enter_context(nc.sbuf_tensor(n, s, d))
            wdn_sb = sb("wdnsb", [128, RK], BF16)
            wup_sb = sb("wupsb", [128, RK], BF16)
            bwb = [sb(f"bwb{i}", [128, D], BF16) for i in range(2)]
            stg = [sb(f"stg8{i}", [128, 1024], BF16) for i in range(16)]
            wofstg = [sb(f"wof8{i}", [128, D], BF16) for i in range(8)]
            pf = [pctx.enter_context(nc.psum_tensor(f"pf8{i}", [128, 512], F32))
                  for i in range(4)]

            with nc.Block("bprep") as blk:
                @blk.gpsimd
                def _(g):
                    swd.inc(g.dma_start(wdn_sb[:], wdn[:]), 16)
                    swd.inc(g.dma_start(wup_sb[:], wupT[:]), 16)
                    for wi, wsrc in enumerate([bWq, bWk, bWoT]):
                        for h in range(NHB):
                            lb = wi * NHB + h
                            par = lb % 2
                            if lb >= 2:
                                g.wait_ge(mmf.h, bb["mmf"] + _fold_cum(lb - 1))
                            row0 = h * DHB
                            swb[par].inc(g.dma_start(
                                bwb[par][:], wsrc[row0:row0 + DHB, :]), 16)

                @blk.vector
                def _(v):
                    inst = None
                    for ic in range(16):
                        inst = v.memset(stg[ic][:], 0.0)
                    for i in range(8):
                        inst = v.memset(wofstg[i][:], 0.0)
                    dvz.inc(inst)

                @blk.tensor
                def _(t):
                    swd.wait(t)
                    for wi in range(3):
                        for h in range(NHB):
                            lb = wi * NHB + h
                            par = lb % 2
                            t.wait_ge(swb[par].h, b_swb[par] + 16 * (lb // 2 + 1))
                            if wi < 2:
                                for ic in range(NIC):
                                    bank = ic % 4
                                    if lb > 0 or ic >= 4:
                                        t.wait_ge(acf.h, bb["acf"] + lb * NIC + ic - 3)
                                    mmf.inc(t.matmul(pf[bank][0:128, 0:RK],
                                                     bwb[par][:, ic * 128:(ic + 1) * 128],
                                                     wdn_sb[:], start=True, stop=True))
                            else:
                                for oc4 in range(4):
                                    bank = oc4 % 4
                                    t.wait_ge(acf.h, bb["acf"] + 2 * NHB * NIC + (lb - 2 * NHB) * 4 + oc4 - 3)
                                    mmf.inc(t.matmul(pf[bank][0:RK, :],
                                                     wup_sb[:],
                                                     bwb[par][:, oc4 * 512:(oc4 + 1) * 512],
                                                     start=True, stop=True))

                @blk.scalar
                def _(s):
                    dvz.wait(s)
                    for wi in range(3):
                        for h in range(NHB):
                            lb = wi * NHB + h
                            if wi < 2:
                                for ic in range(NIC):
                                    s.wait_ge(mmf.h, bb["mmf"] + _fold_cum(lb) + ic + 1)
                                    if wi == 1 and h == 0 and ic == 0:
                                        s.wait_ge(stw.h, bb["stw"] + 16 * 16)
                                    acf.inc(s.copy(stg[ic][:, 64 * h:64 * h + RK],
                                                   pf[ic % 4][0:128, 0:RK]))
                            else:
                                for oc4 in range(4):
                                    s.wait_ge(mmf.h, bb["mmf"] + _fold_cum(lb) + oc4 + 1)
                                    acf.inc(s.copy(
                                        wofstg[h // 2][64 * (h % 2):64 * (h % 2) + RK,
                                                       oc4 * 512:(oc4 + 1) * 512],
                                        pf[oc4 % 4][0:RK, :]))

                @blk.sync
                def _(sy):
                    sy.wait_ge(acf.h, bb["acf"] + NHB * NIC)
                    for ic in range(16):
                        stw.inc(sy.dma_start(
                            WQRD.rearrange("(c p) s -> c p s", p=128)[ic], stg[ic][:]), 16)
                    sy.wait_ge(acf.h, bb["acf"] + 2 * NHB * NIC)
                    for ic in range(16):
                        stwk.inc(sy.dma_start(
                            WKRD.rearrange("(c p) s -> c p s", p=128)[ic], stg[ic][:]), 16)
                    sy.wait_ge(acf.h, bb["acf"] + 2 * NHB * NIC + NHB * 4)
                    for i in range(8):
                        stwo.inc(sy.dma_start(
                            WOFD.rearrange("(c p) o -> c p o", p=128)[i], wofstg[i][:]), 16)

                @blk.gpsimd
                def _(g):
                    for h in range(NHB):
                        lb = 3 * NHB + h
                        par = lb % 2
                        g.wait_ge(mmf.h, bb["mmf"] + _fold_cum(lb - 1))
                        row0 = h * DHB
                        swb[par].inc(g.dma_start(
                            bwb[par][:], bWv[row0:row0 + DHB, :]), 16)

                @blk.tensor
                def _(t):
                    for h in range(NHB):
                        lb = 3 * NHB + h
                        par = lb % 2
                        t.wait_ge(swb[par].h, b_swb[par] + 16 * (lb // 2 + 1))
                        for ic in range(NIC):
                            bank = ic % 4
                            t.wait_ge(acf.h, bb["acf"] + 2 * NHB * NIC + NHB * 4 + (lb - 3 * NHB) * NIC + ic - 3)
                            mmf.inc(t.matmul(pf[bank][0:128, 0:RK],
                                             bwb[par][:, ic * 128:(ic + 1) * 128],
                                             wdn_sb[:], start=True, stop=True))

                @blk.scalar
                def _(s):
                    for h in range(NHB):
                        lb = 3 * NHB + h
                        for ic in range(NIC):
                            s.wait_ge(mmf.h, bb["mmf"] + _fold_cum(lb) + ic + 1)
                            if h == 0 and ic == 0:
                                s.wait_ge(stwk.h, bb["stwk"] + 16 * 16)
                            acf.inc(s.copy(stg[ic][:, 32 * h:32 * h + RK],
                                           pf[ic % 4][0:128, 0:RK]))

                @blk.sync
                def _(sy):
                    sy.wait_ge(acf.h, bb["acf"] + 2 * NHB * NIC + NHB * 4 + NHB * NIC)
                    for ic in range(16):
                        stwv.inc(sy.dma_start(
                            WVRD.rearrange("(c p) s -> c p s", p=128)[ic],
                            stg[ic][:, 0:512]), 16)
                    stw.wait(sy); stwk.wait(sy); stwo.wait(sy); stwv.wait(sy)

        with ExitStack() as pctx:
            sb = lambda n, s, d: pctx.enter_context(nc.sbuf_tensor(n, s, d))
            wsbq = [sb(f"wsbq{i}", [128, 2048], BF16) for i in range(2)]
            krst = [sb(f"krst{i}", [128, 2048], BF16) for i in range(2)]
            wvr_sb = sb("wvrsb", [128, 16 * 512], BF16)
            vstg = [sb(f"vstg8{i}", [128, 528], BF16) for i in range(2)]
            pf = [pctx.enter_context(nc.psum_tensor(f"pf8b{i}", [128, 512], F32))
                  for i in range(4)]

            with nc.Block("bproj") as blk:
                @blk.vector
                def _(v):
                    inst2 = None
                    for i in range(2):
                        inst2 = v.memset(vstg[i][:], 0.0)
                    dvz.inc(inst2)
                    dvz.wait(v)
                    for i in range(2):
                        for h in range(NHB):
                            inst2 = v.memset(vstg[i][0:128, 33 * h + 32:33 * h + 33], 1.0)
                    dvz.inc(inst2)

                @blk.scalar
                def _(s):
                    dvz.wait(s)

                @blk.sync
                def _(sy):
                    stw.wait(sy); stwk.wait(sy); stwo.wait(sy); stwv.wait(sy)

                    def krload(li):
                        src = WQRD if li < 8 else WKRD
                        st = li % 8
                        par = li % 2
                        if li >= 2:
                            sy.wait_ge(mmp.h, bb["mmp"] + (li - 1) * 4)
                        swr[par].inc(sy.dma_start(
                            wsbq[par].rearrange("p (ic s) -> p ic s", s=128),
                            src[:, st * 128:(st + 1) * 128]
                            .rearrange("(ic p) s -> p ic s", p=128)), 16)

                    def krstore(li):
                        dst = QRD if li < 8 else KRIN
                        st = li % 8
                        sy.wait_ge(acp.h, bb["acp"] + li * 4 + 4)
                        stp[li % 2].inc(sy.dma_start(
                            dst.rearrange("(c p) t -> c p t", p=128)[st],
                            krst[li % 2][:]), 16)

                    for li in range(16):
                        krload(li)
                        if li >= 1:
                            krstore(li - 1)
                    krstore(15)
                    swv.inc(sy.dma_start(
                        wvr_sb.rearrange("p (ic s) -> p ic s", s=512),
                        WVRD.rearrange("(ic p) s -> p ic s", p=128)), 16)

                @blk.tensor
                def _(t):
                    for wi in range(2):
                        for st in range(8):
                            li = wi * 8 + st
                            par = li % 2
                            t.wait_ge(swr[par].h, b_swr[par] + 16 * (li // 2 + 1))
                            for tc in range(4):
                                gp = li * 4 + tc
                                bank = gp % 4
                                if gp >= 4:
                                    t.wait_ge(acp.h, bb["acp"] + gp - 3)
                                inst = None
                                for ic in range(NIC):
                                    inst = t.matmul(pf[bank][:],
                                                    wsbq[par][:, ic * 128:(ic + 1) * 128],
                                                    xln[ic][:, tc * 512:(tc + 1) * 512],
                                                    start=(ic == 0), stop=(ic == NIC - 1))
                                mmp.inc(inst)
                    swv.wait(t)
                    for tt in range(16):
                        gp = 64 + tt
                        bank = gp % 4
                        t.wait_ge(acp.h, bb["acp"] + gp - 3)
                        inst = None
                        for ic in range(NIC):
                            inst = t.matmul(pf[bank][:],
                                            xln[ic][:, tt * 128:(tt + 1) * 128],
                                            wvr_sb[:, ic * 512:(ic + 1) * 512],
                                            start=(ic == 0), stop=(ic == NIC - 1))
                        mmp.inc(inst)

                @blk.scalar
                def _(s):
                    for wi in range(2):
                        for st in range(8):
                            li = wi * 8 + st
                            for tc in range(4):
                                gp = li * 4 + tc
                                s.wait_ge(mmp.h, bb["mmp"] + gp + 1)
                                if tc == 0 and li >= 2:
                                    s.wait_ge(stp[li % 2].h, b_stp[li % 2] + 16 * (li // 2))
                                acp.inc(s.copy(krst[li % 2][:, tc * 512:(tc + 1) * 512],
                                               pf[gp % 4][:]))
                    for tt in range(16):
                        gp = 64 + tt
                        s.wait_ge(mmp.h, bb["mmp"] + gp + 1)
                        if tt >= 2:
                            s.wait_ge(stp[tt % 2].h, b_stp[tt % 2] + 16 * (8 + tt // 2))
                        inst = None
                        for h in range(NHB):
                            inst = s.copy(vstg[tt % 2][:, 33 * h:33 * h + RK],
                                          pf[gp % 4][:, 32 * h:32 * h + RK])
                        acp.inc(inst, 1)

                @blk.sync
                def _(sy):
                    for tt in range(16):
                        sy.wait_ge(acp.h, bb["acp"] + 64 + tt + 1)
                        stp[tt % 2].inc(sy.dma_start(
                            VRIN.rearrange("(h tt p) w -> tt p h w", tt=16, p=128)[tt],
                            vstg[tt % 2].rearrange("p (h w) -> p h w", w=33)), 16)

                @blk.gpsimd
                def _(g):
                    stp[0].wait(g)
                    stp[1].wait(g)
                    ccb.inc(g.collective_compute(
                        "AllGather", ALU.bypass,
                        replica_groups=[[0, 1], [2, 3], [4, 5], [6, 7]],
                        ins=[KRIN.ap().opt()], outs=[KROUT.ap().opt()]))
                    g.wait_ge(ccb.h, ccb.v)
                    ccb.inc(g.collective_compute(
                        "AllGather", ALU.bypass,
                        replica_groups=[[0, 1], [2, 3], [4, 5], [6, 7]],
                        ins=[VRIN.ap().opt()], outs=[VROUT.ap().opt()]))

    emit_bprep()
    if upto == PHASES.index("bprep"):
        _finish(nc, mk, x, tout)
        mk.ctx.close()
        return nc

    def emit_bcore():
        ldh = [mk.sem("ldh0"), mk.sem("ldh1")]
        sqs = mk.sem("sqs")
        bse = mk.sem("bse")
        cte = mk.sem("cte")
        ms3 = mk.sem("ms3")
        dvs = mk.sem("dvs")
        mmq = mk.sem("mmq")
        mmav = mk.sem("mmav")
        ac3 = mk.sem("ac3")
        rep = mk.sem("rep")
        bcm = mk.sem("bcm")
        epc = mk.sem("epc")
        ctm = mk.sem("ctm")
        dvz2 = mk.sem("dvz2")
        ccb = mk.sems["ccb"]
        RING = 6
        NHP = 8
        BSC = 0.17677669529663687
        b3 = {n: mk.sems[n].v for n in
              ("sqs", "bse", "cte", "ms3", "dvs", "mmq", "mmav", "ac3",
               "rep", "bcm", "epc", "ctm")}
        b_ldh = [ldh[0].v, ldh[1].v]

        with ExitStack() as pctx:
            sb = lambda n, s, d: pctx.enter_context(nc.sbuf_tensor(n, s, d))
            ktile = [sb(f"kt9{i}", [128, S], BF16) for i in range(2)]
            qtile = [sb(f"qt9{i}", [128, T], BF16) for i in range(2)]
            vbuf = [[sb(f"vb9{i}{h}", [128, 32 * 33], BF16) for h in range(2)]
                    for i in range(2)]
            sqt = sb("sqt9", [128, T], BF16)
            probs = sb("probs9", [128, 3 * 1024], BF16)
            ctxr = sb("ctxr9", [32, 512], BF16)
            ctx_st = [sb(f"cst9{i}", [128, 512], BF16) for i in range(4)]
            mstat = sb("mstat9", [1, 40], F32)
            rrow = sb("rrow9", [1, 512], F32)
            bias_h = [sb(f"bh9{i}", [128, 1], F32) for i in range(2)]
            qk_ps = [pctx.enter_context(nc.psum_tensor(f"qk9{i}", [128, 1024], F32))
                     for i in range(2)]
            av_ps = pctx.enter_context(nc.psum_tensor("av9", [33, 512], F32))
            bc_ps = pctx.enter_context(nc.psum_tensor("bc9", [32, 512], F32))
            st_ps = [qk_ps[0][0:1, 0:512], qk_ps[0][0:1, 512:1024],
                     qk_ps[1][0:1, 0:512], qk_ps[1][0:1, 512:1024]]
            stb_ps = [qk_ps[0][0:128, 0:1], qk_ps[1][0:128, 0:1]]

            ROUNDS = [(0, 0, 0), (1, 0, 12), (0, 1, 4), (1, 1, 16), (0, 2, 8), (1, 2, 20)]

            def widx(hp, hloc, qc, kt=0):
                return hp * 256 + (hloc * 4 + qc) * 32 + kt

            def grp(hp, hloc, qc):
                return hp * 8 + hloc * 4 + qc

            with nc.Block("bcore") as blk:
                @blk.sync
                def _(sy):
                    def stores(hp):
                        sy.wait_ge(epc.h, b3["epc"] + hp * 8 + 8)
                        for qc in range(4):
                            ctm.inc(sy.dma_start(
                                CTXD[hp * 128:(hp + 1) * 128, qc * 512:(qc + 1) * 512],
                                ctx_st[qc][:]), 16)

                    for hp in range(NHP):
                        par = hp % 2
                        if hp >= 2:
                            sy.wait_ge(mmav.h, b3["mmav"] + (hp - 1) * 256)
                            sy.wait_ge(sqs.h, b3["sqs"] + (hp - 1) * 5)
                        if hp == 0:
                            ccb.wait(sy)
                        ldh[par].inc(sy.dma_start(
                            qtile[par][:], QRD[hp * 128:(hp + 1) * 128, :]), 16)
                        for r in range(2):
                            ldh[par].inc(sy.dma_start(
                                ktile[par][:, r * T:(r + 1) * T],
                                KROUT[r * 1024 + hp * 128: r * 1024 + (hp + 1) * 128, :]), 16)
                        for hloc in range(2):
                            for r in range(2):
                                src = VROUT.rearrange("(r h tt p) w -> r h p tt w",
                                                      r=2, h=NHB, p=128)[r, 2 * hp + hloc]
                                ldh[par].inc(sy.dma_start(
                                    vbuf[par][hloc].rearrange("p (kt w) -> p kt w", w=33)
                                    [:, r * 16:(r + 1) * 16, :],
                                    src), 16)
                        if hp >= 1:
                            stores(hp - 1)
                    stores(NHP - 1)
                    ctm.wait(sy)

                @blk.vector
                def _(v):
                    inst = None
                    for qc in range(4):
                        inst = v.memset(ctx_st[qc][32:64, :], 0.0)
                        inst = v.memset(ctx_st[qc][96:128, :], 0.0)
                    dvz2.inc(inst)
                    for hp in range(NHP):
                        for r, (hloc, sqi, colb) in enumerate(ROUNDS):
                            v.wait_ge(ms3.h, b3["ms3"] + hp * 7 + r + 1)
                            inst = None
                            for tc in range(4):
                                inst = v.reduce_max(mstat[0:1, colb + tc:colb + tc + 1],
                                                    st_ps[tc], AX.X)
                            dvs.inc(inst)
                        dvs.wait(v)
                        inst = None
                        for hloc in range(2):
                            qb = hloc * 12
                            v.reduce_max(mstat[0:1, 24 + 2 * hloc:25 + 2 * hloc],
                                         mstat[0:1, qb:qb + 4], AX.X)
                            inst = v.reduce_max(mstat[0:1, 25 + 2 * hloc:26 + 2 * hloc],
                                                mstat[0:1, qb + 4:qb + 12], AX.X)
                        dvs.inc(inst)
                        v.wait_ge(sqs.h, b3["sqs"] + hp * 5 + 4)
                        v.tensor_tensor(mstat[0:1, 32:33], mstat[0:1, 28:29],
                                        mstat[0:1, 29:30], ALU.add)
                        dvs.inc(v.tensor_tensor(mstat[0:1, 33:34], mstat[0:1, 30:31],
                                                mstat[0:1, 31:32], ALU.add))
                        v.wait_ge(sqs.h, b3["sqs"] + hp * 5 + 5)
                        dvs.inc(v.tensor_scalar(mstat[0:1, 36:38], mstat[0:1, 34:36],
                                                -BSC, None, ALU.mult))
                        for hloc in range(2):
                            for qc in range(4):
                                g_ = grp(hp, hloc, qc)
                                v.wait_ge(mmav.h, b3["mmav"] + (g_ + 1) * 32)
                                if g_ >= 1:
                                    v.wait_ge(bcm.h, b3["bcm"] + g_)
                                rep.inc(v.reciprocal(rrow[:], av_ps[32:33, :]))
                                v.wait_ge(bcm.h, b3["bcm"] + g_ + 1)
                                v.wait_ge(cte.h, b3["cte"] + g_ + 1)
                                if hloc == 0 and qc == 0:
                                    v.wait_ge(ctm.h, b3["ctm"] + hp * 64)
                                epc.inc(v.tensor_tensor(
                                    ctx_st[qc][64 * hloc:64 * hloc + 32, :],
                                    ctxr[:], bc_ps[:], ALU.mult))

                @blk.scalar
                def _(s):
                    for hp in range(NHP):
                        par = hp % 2
                        s.wait_ge(ldh[par].h, b_ldh[par] + 112 * (hp // 2 + 1))
                        if hp >= 1:
                            s.wait_ge(ms3.h, b3["ms3"] + hp * 7 - 1)
                        sqs.inc(s.activation(sqt[:], qtile[par][:], AF.Square))
                        s.wait_ge(ms3.h, b3["ms3"] + hp * 7 + 2)
                        sqs.inc(s.activation(sqt[:], ktile[par][:, 0:T], AF.Square))
                        s.wait_ge(ms3.h, b3["ms3"] + hp * 7 + 4)
                        sqs.inc(s.activation(sqt[:], ktile[par][:, T:S], AF.Square))
                        s.wait_ge(dvs.h, b3["dvs"] + hp * 9 + 7)
                        sqs.inc(s.activation(mstat[0:1, 28:32], mstat[0:1, 24:28], AF.Ln))
                        s.wait_ge(dvs.h, b3["dvs"] + hp * 9 + 8)
                        sqs.inc(s.activation(mstat[0:1, 34:36], mstat[0:1, 32:34],
                                             AF.Exp, scale=0.5))
                        s.wait_ge(ms3.h, b3["ms3"] + hp * 7 + 7)
                        if hp >= 1:
                            s.wait_ge(ac3.h, b3["ac3"] + hp * 128)
                        bse.inc(s.copy(bias_h[0][:], stb_ps[0]))
                        bse.inc(s.copy(bias_h[1][:], stb_ps[1]))
                        for hloc in range(2):
                            for qc in range(4):
                                g_ = grp(hp, hloc, qc)
                                for pr in range(16):
                                    idx0 = widx(hp, hloc, qc, 2 * pr)
                                    pidx = idx0 // 2
                                    s.wait_ge(mmq.h, b3["mmq"] + idx0 + 2)
                                    if pidx >= 3:
                                        s.wait_ge(mmav.h, b3["mmav"] + (pidx - 3) * 2 + 2)
                                    slot = pidx % 3
                                    ac3.inc(s.activation(
                                        probs[:, slot * 1024:(slot + 1) * 1024],
                                        qk_ps[pidx % 2][:], AF.Exp, scale=BSC,
                                        bias=bias_h[hloc][:]))
                                s.wait_ge(mmav.h, b3["mmav"] + (g_ + 1) * 32)
                                if g_ >= 1:
                                    s.wait_ge(epc.h, b3["epc"] + g_)
                                cte.inc(s.copy(ctxr[:], av_ps[0:32, :]))

                @blk.tensor
                def _(t):
                    dvz2.wait(t)
                    for hp in range(NHP):
                        par = hp % 2
                        for r, (hloc, sqi, colb) in enumerate(ROUNDS):
                            t.wait_ge(sqs.h, b3["sqs"] + hp * 5 + sqi + 1)
                            if r >= 1:
                                t.wait_ge(dvs.h, b3["dvs"] + hp * 9 + r)
                            inst = None
                            for tc in range(4):
                                inst = t.matmul(st_ps[tc],
                                                oheads[:, hloc:hloc + 1],
                                                sqt[:, tc * 512:(tc + 1) * 512],
                                                start=True, stop=True)
                            ms3.inc(inst)
                        t.wait_ge(dvs.h, b3["dvs"] + hp * 9 + 9)
                        if hp >= 1:
                            t.wait_ge(bse.h, b3["bse"] + hp * 2)
                        t.matmul(stb_ps[0], onesPf[:], mstat[0:1, 36:37],
                                 start=True, stop=True)
                        ms3.inc(t.matmul(stb_ps[1], onesPf[:], mstat[0:1, 37:38],
                                         start=True, stop=True))
                        for hloc in range(2):
                            hb = 64 * hloc
                            for qc in range(4):
                                g_ = grp(hp, hloc, qc)

                                def av(kt):
                                    idx = widx(hp, hloc, qc, kt)
                                    pidx = idx // 2
                                    t.wait_ge(ac3.h, b3["ac3"] + pidx + 1)
                                    if kt == 0 and g_ >= 1:
                                        t.wait_ge(rep.h, b3["rep"] + g_)
                                        t.wait_ge(cte.h, b3["cte"] + g_)
                                    mmav.inc(t.matmul(
                                        av_ps[:],
                                        vbuf[par][hloc][:, kt * 33:kt * 33 + 33],
                                        probs[:, (pidx % 3) * 1024 + (kt % 2) * 512:
                                              (pidx % 3) * 1024 + (kt % 2) * 512 + 512],
                                        start=(kt == 0), stop=(kt == 31)))

                                for kt in range(32):
                                    idx = widx(hp, hloc, qc, kt)
                                    pidx = idx // 2
                                    if kt % 2 == 0 and pidx >= 2:
                                        t.wait_ge(ac3.h, b3["ac3"] + pidx - 1)
                                    if hloc == 0 and qc == 0 and kt == 0:
                                        t.wait_ge(bse.h, b3["bse"] + hp * 2 + 2)
                                    mmq.inc(t.matmul(
                                        qk_ps[pidx % 2][:, (kt % 2) * 512:(kt % 2) * 512 + 512],
                                        ktile[par][hb:hb + 64, kt * 128:(kt + 1) * 128],
                                        qtile[par][hb:hb + 64, qc * 512:(qc + 1) * 512],
                                        start=True, stop=True))
                                    if kt >= 3:
                                        av(kt - 3)
                                av(29)
                                av(30)
                                av(31)
                                t.wait_ge(rep.h, b3["rep"] + g_ + 1)
                                if g_ >= 1:
                                    t.wait_ge(epc.h, b3["epc"] + g_)
                                bcm.inc(t.matmul(bc_ps[:], onesPf[0:1, 0:32], rrow[:],
                                                 start=True, stop=True))

    emit_bcore()
    if upto == PHASES.index("bcore"):
        _finish(nc, mk, x, tout)
        mk.ctx.close()
        return nc

    def emit_bout():
        ldw = [mk.sem("ldw0"), mk.sem("ldw1")]
        ldc = [mk.sem("ldc0"), mk.sem("ldc1")]
        mmo = mk.sem("mmo")
        dvo = mk.sem("dvo")
        b_mmo, b_dvo = mmo.v, dvo.v
        b_ldw = [ldw[0].v, ldw[1].v]
        b_ldc = [ldc[0].v, ldc[1].v]
        with ExitStack() as pctx:
            sb = lambda n, s, d: pctx.enter_context(nc.sbuf_tensor(n, s, d))
            wob = [sb(f"wob{i}", [128, 8 * 128], BF16) for i in range(2)]
            cbuf = [sb(f"cbb{i}", [128, 8 * 512], BF16) for i in range(2)]
            po = [pctx.enter_context(nc.psum_tensor(f"pob{i}", [128, 512], F32))
                  for i in range(4)]
            with nc.Block("bout") as blk:
                @blk.sync
                def _(sy):
                    for tc in range(4):
                        if tc >= 2:
                            sy.wait_ge(mmo.h, b_mmo + (tc - 1) * 16)
                        ldc[tc % 2].inc(sy.dma_start(
                            cbuf[tc % 2][:],
                            CTXD[0:1024, tc * 512:(tc + 1) * 512]
                            .rearrange("(sc p) t -> p sc t", p=128)), 16)
                        for oc in range(16):
                            li = tc * 16 + oc
                            if li >= 2:
                                sy.wait_ge(mmo.h, b_mmo + li - 1)
                            ldw[li % 2].inc(sy.dma_start(
                                wob[li % 2].rearrange("p (sc o) -> p sc o", o=128),
                                WOFD[:, oc * 128:(oc + 1) * 128]
                                .rearrange("(sc p) o -> p sc o", p=128)), 16)

                @blk.tensor
                def _(t):
                    for tc in range(4):
                        t.wait_ge(ldc[tc % 2].h, b_ldc[tc % 2] + 16 * (tc // 2 + 1))
                        for oc in range(16):
                            li = tc * 16 + oc
                            t.wait_ge(ldw[li % 2].h, b_ldw[li % 2] + 16 * (li // 2 + 1))
                            if li >= 4:
                                t.wait_ge(dvo.h, b_dvo + li - 3)
                            inst = None
                            for sc in range(8):
                                inst = t.matmul(po[li % 4][:],
                                                wob[li % 2][:, sc * 128:(sc + 1) * 128],
                                                cbuf[tc % 2][:, sc * 512:(sc + 1) * 512],
                                                start=(sc == 0), stop=(sc == 7))
                            mmo.inc(inst)

                @blk.vector
                def _(v):
                    for tc in range(4):
                        for oc in range(16):
                            li = tc * 16 + oc
                            v.wait_ge(mmo.h, b_mmo + li + 1)
                            sl = slice(tc * 512, (tc + 1) * 512)
                            dvo.inc(v.tensor_tensor(x[oc][:, sl], x[oc][:, sl],
                                                    po[li % 4][:], ALU.add))

    emit_bout()
    if upto == PHASES.index("bout"):
        _finish(nc, mk, x, tout)
        mk.ctx.close()
        return nc

    emit_ln("4", x, xln)
    if upto == PHASES.index("ln4"):
        _finish(nc, mk, xln, tout)
        mk.ctx.close()
        return nc

    emit_ffn("f2")

    _finish(nc, mk, x, tout)
    mk.ctx.close()
    return nc


def _finish(nc, mk, src, tout):
    stf = [mk.sem("stf0"), mk.sem("stf1")]
    evf = mk.sem("evf")
    b_evf = evf.v
    with ExitStack() as pctx:
        xf = [pctx.enter_context(nc.sbuf_tensor(f"xf{i}", [128, T], F32)) for i in range(2)]
        with nc.Block("finish") as blk:
            @blk.scalar
            def _(s):
                for i in range(NIC):
                    if i >= 2:
                        s.wait_ge(stf[i % 2].h, 16 * (i // 2))
                    evf.inc(s.copy(xf[i % 2][:], src[i][:]))

            @blk.sync
            def _(sy):
                for i in range(NIC):
                    sy.wait_ge(evf.h, b_evf + i + 1)
                    stf[i % 2].inc(sy.dma_start(
                        tout.rearrange("(c p) t -> c p t", p=128)[i], xf[i % 2][:]), 16)
                stf[0].wait(sy)
                stf[1].wait(sy)


_PATCHED = [False]


def _patch_fast_pjrt():
    if _PATCHED[0]:
        return
    import jax
    from jax.sharding import Mesh, PartitionSpec, NamedSharding
    from jax.experimental.shard_map import shard_map
    import concourse.bass2jax as b2j
    from concourse.bass2jax import _bass_exec_p, install_neuronx_cc_hook, partition_id_tensor

    def fast_run(nc, in_maps, n_cores):
        install_neuronx_cc_hook()
        pn = nc.partition_id_tensor.name if nc.partition_id_tensor else None
        in_names, out_names, out_avals = [], [], []
        zero_shapes = []
        for alloc in nc.m.functions[0].allocations:
            if not isinstance(alloc, mybir.MemoryLocationSet):
                continue
            name = alloc.memorylocations[0].name
            if alloc.kind == "ExternalInput":
                if name != pn:
                    in_names.append(name)
            elif alloc.kind == "ExternalOutput":
                out_names.append(name)
                shape = tuple(alloc.tensor_shape)
                dtype = mybir.dt.np(alloc.dtype)
                out_avals.append(jax.core.ShapedArray(shape, dtype))
                zero_shapes.append((shape, dtype))
        n_params = len(in_names)
        n_outs = len(out_avals)
        all_names = list(in_names) + list(out_names)
        if pn:
            all_names.append(pn)

        def _body(*args):
            operands = list(args)
            if pn:
                operands.append(partition_id_tensor())
            outs = _bass_exec_p.bind(
                *operands, out_avals=tuple(out_avals), in_names=tuple(all_names),
                out_names=tuple(out_names), lowering_input_output_aliases=(),
                sim_require_finite=True, sim_require_nnan=True, nc=nc)
            return tuple(outs)

        devices = jax.devices()[:n_cores]
        donate = tuple(range(n_params, n_params + n_outs))
        if n_cores == 1:
            args = [np.ascontiguousarray(in_maps[0][n]) for n in in_names]
            args += [np.zeros(s, d) for s, d in zero_shapes]
            dargs = [jax.device_put(a, devices[0]) for a in args]
            for a in dargs:
                a.block_until_ready()
            out_arrs = jax.jit(_body, donate_argnums=donate, keep_unused=True)(*dargs)
            return [{n: np.asarray(out_arrs[i]) for i, n in enumerate(out_names)}]

        mesh = Mesh(np.asarray(devices), ("core",))
        sharding = NamedSharding(mesh, PartitionSpec("core"))
        in_specs = (PartitionSpec("core"),) * (n_params + n_outs)
        out_specs = (PartitionSpec("core"),) * n_outs
        sharded = jax.jit(
            shard_map(_body, mesh=mesh, in_specs=in_specs, out_specs=out_specs,
                      check_rep=False),
            donate_argnums=donate, keep_unused=True)
        global_in = []
        for n in in_names:
            shards = [jax.device_put(np.ascontiguousarray(in_maps[c][n]), devices[c])
                      for c in range(n_cores)]
            gshape = (n_cores * shards[0].shape[0],) + tuple(shards[0].shape[1:])
            global_in.append(jax.make_array_from_single_device_arrays(gshape, sharding, shards))
        for s, d in zero_shapes:
            shards = [jax.device_put(np.zeros(s, d), dev) for dev in devices]
            gshape = (n_cores * s[0],) + tuple(s[1:])
            global_in.append(jax.make_array_from_single_device_arrays(gshape, sharding, shards))
        out_arrs = sharded(*global_in)
        res = []
        for c in range(n_cores):
            d = {}
            for i, n in enumerate(out_names):
                a = np.asarray(out_arrs[i]).reshape(n_cores, *out_avals[i].shape)
                d[n] = a[c]
            res.append(d)
        return res

    b2j.run_bass_via_pjrt = fast_run
    _PATCHED[0] = True



def _tile_w13(w):
    z = np.zeros((len(HCH), 128, 16, 128), dtype=np.float32)
    for hb, (hb0, hn) in enumerate(HCH):
        blk = w[hb0:hb0 + hn, :]
        z[hb, :, :, 0:hn] = blk.T.reshape(16, 128, hn).transpose(1, 0, 2)
    return np.ascontiguousarray(z.reshape(len(HCH), 128, 16 * 128))


def _tile_w2(w):
    z = np.zeros((16, 128, len(HCH), 128), dtype=np.float32)
    for hc, (hb0, hn) in enumerate(HCH):
        blk = w[:, hb0:hb0 + hn]
        z[:, 0:hn, hc, :] = blk.T.reshape(hn, 16, 128).transpose(1, 0, 2)
    return np.ascontiguousarray(z.reshape(16, 128, len(HCH) * 128))


def stage_inputs(inputs, declared):
    c = np.ascontiguousarray
    x = np.asarray(inputs["L0_attn_a_hidden_states"])
    common = {
        "wqT": lambda: c(np.asarray(inputs["L0_attn_a_wq"]).T),
        "wkT": lambda: c(np.asarray(inputs["L0_attn_a_wk"]).T),
        "wvT": lambda: c(np.asarray(inputs["L0_attn_a_wv"]).T),
        "woT": lambda: c(np.asarray(inputs["L0_attn_a_wo"]).T),
        "relT": lambda: c(np.asarray(inputs["L0_attn_a_rel_pos"]).T),
        "bWq": lambda: c(np.asarray(inputs["L0_attn_b_Wq"])),
        "bWk": lambda: c(np.asarray(inputs["L0_attn_b_Wk"])),
        "bWv": lambda: c(np.asarray(inputs["L0_attn_b_Wv"])),
        "bWoT": lambda: c(np.asarray(inputs["L0_attn_b_Wo"]).T),
        "wdn": lambda: c(np.asarray(inputs["L0_attn_b_Wdown"])),
        "wupT": lambda: c(np.asarray(inputs["L0_attn_b_Wup"]).T),
        "sw1R": lambda: _tile_w13(np.asarray(inputs["L0_ffn_sw1"])),
        "sw3R": lambda: _tile_w13(np.asarray(inputs["L0_ffn_sw3"])),
        "sw2R": lambda: _tile_w2(np.asarray(inputs["L0_ffn_sw2"])),
        "ew1R": lambda: np.stack([_tile_w13(np.asarray(inputs["L0_ffn_ew1"])[e])
                                  for e in range(NE)]),
        "ew3R": lambda: np.stack([_tile_w13(np.asarray(inputs["L0_ffn_ew3"])[e])
                                  for e in range(NE)]),
        "ew2R": lambda: np.stack([_tile_w2(np.asarray(inputs["L0_ffn_ew2"])[e])
                                  for e in range(NE)]),
        "iden": lambda: np.eye(128, dtype=np.float32),
    }

    def _wrtp():
        w = np.zeros((D, 64), dtype=np.float32)
        w[:, 0:NE] = np.asarray(inputs["L0_ffn_wr"]).T
        return c(w)
    common["wrTp"] = _wrtp
    shared = {k: f() for k, f in common.items() if k in declared}
    in_maps = []
    for core in range(NCORES):
        b, h = core // 2, core % 2
        m = dict(shared)
        if "xin" in declared:
            m["xin"] = c(x[b, h * T:(h + 1) * T, :].T)
        if "halfin" in declared:
            m["halfin"] = np.array([[float(h)]], dtype=np.float32)
        in_maps.append(m)
    return in_maps


def kernel(**inputs):
    _patch_fast_pjrt()
    nc = build_nc(TRUNCATE)
    declared = set()
    for alloc in nc.m.functions[0].allocations:
        if isinstance(alloc, mybir.MemoryLocationSet) and alloc.kind == "ExternalInput":
            declared.add(alloc.memorylocations[0].name)
    in_maps = stage_inputs(inputs, declared)
    res = run_bass_kernel_spmd(nc, in_maps, core_ids=list(range(NCORES)))
    x = np.asarray(inputs["L0_attn_a_hidden_states"])
    out = np.empty((4, S, D), dtype=np.float32)
    for core in range(NCORES):
        b, h = core // 2, core % 2
        out[b, h * T:(h + 1) * T, :] = res.results[core]["tout"].T
    return out
